# version 15
# baseline (speedup 1.0000x reference)
import contextlib
import numpy as np
import ml_dtypes

import concourse.bass as bass
import concourse.mybir as mybir
from concourse.bass_utils import run_bass_kernel_spmd

F32 = mybir.dt.float32
BF16 = mybir.dt.bfloat16
I32 = mybir.dt.int32
AF = mybir.ActivationFunctionType
ALU = mybir.AluOpType
AX = mybir.AxisListType

B, S, D = 4, 4096, 2048
NH, HD = 16, 128
DFF = 5632
NCORE = 8
QSC = 1.0 / float(np.sqrt(128.0))
INV127 = float(np.float32(1.0 / 127.0))
EPS_LN = 1e-5
EPS_Q = 1e-8
NT = S // 128
OS = S // 2
ONT = OS // 128
NIB = DFF // 512
NIC = DFF // 128

DEBUG = False


class Ctr:
    def __init__(self, h):
        self.h = h
        self.n = 0

    def bump(self, k):
        self.n += k
        return self.n


class Plan:
    ENGINES = ("sync", "tensor", "scalar", "vector", "gpsimd")

    def __init__(self):
        self.ops = {e: [] for e in self.ENGINES}

    def add(self, eng, emit, waits=(), inc=None, inck=1):
        v = inc.bump(inck) if inc is not None else None
        self.ops[eng].append((emit, [w for w in waits if w is not None], inc, inck))
        return v

    def emit(self, nc):
        with nc.Block() as block:
            for eng_name in self.ENGINES:
                ops = self.ops[eng_name]
                if not ops:
                    continue

                def make(ops_local):
                    def body(eng):
                        for emit, waits, inc, inck in ops_local:
                            for (c, v) in waits:
                                eng.wait_ge(c.h, v)
                            inst = emit(eng)
                            if inc is not None:
                                inst.then_inc(inc.h, inck)
                    return body

                getattr(block, eng_name)(make(ops))


def build_nc():
    nc = bass.Bass("TRN2", target_bir_lowering=False)

    x_in = nc.dram_tensor("x", [S, D], F32, kind="ExternalInput").ap()
    wqkv = nc.dram_tensor("wqkv", [D, 3 * D], F32, kind="ExternalInput").ap()
    bqkv2 = nc.dram_tensor("bqkv2", [128, 48], F32, kind="ExternalInput").ap()
    wo_in = nc.dram_tensor("wo", [D, D], F32, kind="ExternalInput").ap()
    bo_t = nc.dram_tensor("bo", [1, D], F32, kind="ExternalInput")
    bo_in = bo_t.ap()
    w1_in = nc.dram_tensor("w1", [2 * DFF, D], F32, kind="ExternalInput").ap()
    w2_in = nc.dram_tensor("w2", [D, DFF], F32, kind="ExternalInput").ap()
    masks_in = nc.dram_tensor("masks", [6, 128, 512], BF16, kind="ExternalInput").ap()
    out = nc.dram_tensor("out", [OS, D], F32, kind="ExternalOutput").ap()

    xn_tok = nc.dram_tensor("xn_tok", [S, D], BF16, kind="Internal").ap()
    qx_st = nc.dram_tensor("qx_st", [OS, D], BF16, kind="Internal").ap()
    kqvT = nc.dram_tensor("kqvT", [3 * D, S], BF16, kind="Internal").ap()
    otT = nc.dram_tensor("otT", [D, OS], BF16, kind="Internal").ap()
    abuf = nc.dram_tensor("abuf", [OS, D], F32, kind="Internal").ap()
    h_st = nc.dram_tensor("h_st", [OS, DFF], BF16, kind="Internal").ap()
    qh_st = nc.dram_tensor("qh_st", [OS, DFF], BF16, kind="Internal").ap()
    w1q_st = nc.dram_tensor("w1q_st", [2 * DFF, D], BF16, kind="Internal").ap()
    w2q_st = nc.dram_tensor("w2q_st", [D, DFF], BF16, kind="Internal").ap()
    s1d_t = nc.dram_tensor("s1d", [2 * DFF], F32, kind="Internal")
    s2d_t = nc.dram_tensor("s2d", [D], F32, kind="Internal")

    dbg = {}
    if DEBUG:
        for nm, src, shp, dt in [
            ("d_xn", xn_tok, [S, D], BF16), ("d_qx", qx_st, [OS, D], BF16),
            ("d_kqvT", kqvT, [3 * D, S], BF16), ("d_otT", otT, [D, OS], BF16),
            ("d_abuf", abuf, [OS, D], F32), ("d_qh", qh_st, [OS, DFF], BF16),
            ("d_w1q", w1q_st, [2 * DFF, D], BF16),
            ("d_w2q", w2q_st, [D, DFF], BF16),
            ("d_h", h_st, [OS, DFF], BF16),
        ]:
            dbg[nm] = (nc.dram_tensor(nm, shp, dt, kind="ExternalOutput").ap(), src)
        dbg["d_s1"] = (nc.dram_tensor("d_s1", [2 * DFF], F32, kind="ExternalOutput").ap(), s1d_t.ap())
        dbg["d_s2"] = (nc.dram_tensor("d_s2", [D], F32, kind="ExternalOutput").ap(), s2d_t.ap())

    with contextlib.ExitStack() as top:
        LA = Ctr(top.enter_context(nc.semaphore("LA")))
        LB = Ctr(top.enter_context(nc.semaphore("LB")))
        LC = Ctr(top.enter_context(nc.semaphore("LC")))
        LD = Ctr(top.enter_context(nc.semaphore("LD")))
        LE = Ctr(top.enter_context(nc.semaphore("LE")))
        LF = Ctr(top.enter_context(nc.semaphore("LF")))
        TA = Ctr(top.enter_context(nc.semaphore("TA")))
        TB = Ctr(top.enter_context(nc.semaphore("TB")))
        TC = Ctr(top.enter_context(nc.semaphore("TC")))
        TD = Ctr(top.enter_context(nc.semaphore("TD")))
        TE = Ctr(top.enter_context(nc.semaphore("TE")))
        TF = Ctr(top.enter_context(nc.semaphore("TF")))
        LI = Ctr(top.enter_context(nc.semaphore("LI")))
        LJ = Ctr(top.enter_context(nc.semaphore("LJ")))
        sPE = Ctr(top.enter_context(nc.semaphore("sPE")))
        sACT = Ctr(top.enter_context(nc.semaphore("sACT")))
        sDVE = Ctr(top.enter_context(nc.semaphore("sDVE")))
        sPOOL = Ctr(top.enter_context(nc.semaphore("sPOOL")))
        STORES = (TA, TB, TC, TD, TE, TF)

        def entry():
            return [(c, c.n) for c in STORES if c.n > 0]

        bq_sb = top.enter_context(nc.sbuf_tensor("bq_sb", [128, 48], F32))
        bqs_sb = top.enter_context(nc.sbuf_tensor("bqs_sb", [128, 48], F32))
        masks_sb = top.enter_context(nc.sbuf_tensor("masks_sb", [128, 6 * 512], BF16))
        ones_c = top.enter_context(nc.sbuf_tensor("ones_c", [128, 1], F32))
        ones_r = top.enter_context(nc.sbuf_tensor("ones_r", [1, 128], F32))
        ones_bb = top.enter_context(nc.sbuf_tensor("ones_bb", [128, 128], BF16))
        sx_all = top.enter_context(nc.sbuf_tensor("sx_all", [128, NT], F32))
        sh_all = top.enter_context(nc.sbuf_tensor("sh_all", [128, NT], F32))
        eps_c = top.enter_context(nc.sbuf_tensor("eps_c", [128, 1], F32))
        prb6 = top.enter_context(nc.sbuf_tensor("prb6", [128, 2], F32))
        prbi = top.enter_context(nc.sbuf_tensor("prbi", [128, 2], I32))
        prbf = top.enter_context(nc.sbuf_tensor("prbf", [128, 2], F32))
        qr_sc = top.enter_context(nc.sbuf_tensor("qr_sc", [128, 4], F32))
        ab_sb = top.enter_context(nc.sbuf_tensor("ab_sb", [128, 4], F32))

        p = Plan()
        p.add("sync", lambda e: e.dma_start(out=bq_sb[:], in_=bqkv2), inc=LA, inck=16)
        for d_ in range(6):
            p.add("sync", lambda e, d_=d_: e.dma_start(
                out=masks_sb[:, d_ * 512:(d_ + 1) * 512], in_=masks_in[d_]),
                inc=LA, inck=16)
        t_ld = LA.n
        p.add("vector", lambda e: e.memset(ones_c[:], 1.0))
        p.add("vector", lambda e: e.memset(ones_r[:], 1.0))
        p.add("vector", lambda e: e.memset(ones_bb[:], 1.0))
        p.add("vector", lambda e: e.memset(eps_c[:], EPS_LN))
        p.add("vector", lambda e: e.tensor_scalar(
            out=bqs_sb[:, 0:16], in0=bq_sb[:, 0:16], scalar1=QSC, scalar2=None,
            op0=ALU.mult), waits=[(LA, t_ld)])
        p.add("vector", lambda e: e.tensor_copy(out=bqs_sb[:, 16:48], in_=bq_sb[:, 16:48]),
              waits=[(LA, t_ld)])
        g1 = p.add("vector", lambda e: e.memset(prb6[:, 0:1], 0.6), inc=sDVE)
        g2 = p.add("vector", lambda e: e.memset(prb6[:, 1:2], -0.6),
                   waits=[(sDVE, g1)], inc=sDVE)
        g3 = p.add("vector", lambda e: e.tensor_scalar(
            out=prbi[:], in0=prb6[:], scalar1=0.0, scalar2=None, op0=ALU.add),
            waits=[(sDVE, g2)], inc=sDVE)
        g4 = p.add("vector", lambda e: e.tensor_copy(out=prbf[:], in_=prbi[:]),
                   waits=[(sDVE, g3)], inc=sDVE)
        g5 = p.add("vector", lambda e: e.tensor_scalar(
            out=qr_sc[:, 0:1], in0=prbf[:, 0:1], scalar1=-1.0, scalar2=1.0,
            op0=ALU.mult, op1=ALU.add), waits=[(sDVE, g4)], inc=sDVE)
        g6 = p.add("vector", lambda e: e.tensor_scalar(
            out=qr_sc[:, 1:2], in0=prbf[:, 1:2], scalar1=-1.0, scalar2=None,
            op0=ALU.mult), waits=[(sDVE, g5)], inc=sDVE)
        g7 = p.add("vector", lambda e: e.tensor_scalar(
            out=qr_sc[:, 2:3], in0=qr_sc[:, 1:2], scalar1=-1.0, scalar2=1.0,
            op0=ALU.mult, op1=ALU.add), waits=[(sDVE, g6)], inc=sDVE)
        g8 = p.add("vector", lambda e: e.tensor_mul(
            out=qr_sc[:, 3:4], in0=qr_sc[:, 0:1], in1=qr_sc[:, 1:2]),
            waits=[(sDVE, g7)], inc=sDVE)
        g9 = p.add("vector", lambda e: e.tensor_mul(
            out=ab_sb[:, 0:1], in0=qr_sc[:, 0:1], in1=qr_sc[:, 2:3]),
            waits=[(sDVE, g8)], inc=sDVE)
        g10 = p.add("vector", lambda e: e.tensor_scalar(
            out=ab_sb[:, 0:1], in0=ab_sb[:, 0:1], scalar1=0.5, scalar2=None,
            op0=ALU.mult), waits=[(sDVE, g9)], inc=sDVE)
        g11 = p.add("vector", lambda e: e.tensor_scalar(
            out=ab_sb[:, 1:2], in0=qr_sc[:, 3:4], scalar1=0.5, scalar2=None,
            op0=ALU.mult), waits=[(sDVE, g10)], inc=sDVE)
        g12 = p.add("vector", lambda e: e.tensor_scalar(
            out=ab_sb[:, 2:3], in0=ab_sb[:, 0:1], scalar1=2.0, scalar2=None,
            op0=ALU.mult), waits=[(sDVE, g11)], inc=sDVE)
        g13 = p.add("vector", lambda e: e.tensor_sub(
            out=ab_sb[:, 3:4], in0=ab_sb[:, 1:2], in1=ab_sb[:, 0:1]),
            waits=[(sDVE, g12)], inc=sDVE)
        p.emit(nc)
        al_ap = ab_sb[:, 0:1]
        be_ap = ab_sb[:, 1:2]
        ga_ap = ab_sb[:, 2:3]
        dm_ap = ab_sb[:, 3:4]

        with contextlib.ExitStack() as es:
            xt = [es.enter_context(nc.sbuf_tensor(f"s1_x{i}", [128, D], F32)) for i in range(2)]
            x2 = [es.enter_context(nc.sbuf_tensor(f"s1_x2{i}", [128, D], F32)) for i in range(2)]
            s1s = es.enter_context(nc.sbuf_tensor("s1_s1", [128, NT], F32))
            s2s = es.enter_context(nc.sbuf_tensor("s1_s2", [128, NT], F32))
            negm = es.enter_context(nc.sbuf_tensor("s1_negm", [128, NT], F32))
            m2 = es.enter_context(nc.sbuf_tensor("s1_m2", [128, NT], F32))
            varb = es.enter_context(nc.sbuf_tensor("s1_var", [128, NT], F32))
            sdv = es.enter_context(nc.sbuf_tensor("s1_sd", [128, NT], F32))
            rstd = es.enter_context(nc.sbuf_tensor("s1_rstd", [128, NT], F32))
            nmr = es.enter_context(nc.sbuf_tensor("s1_nmr", [128, NT], F32))
            xnf = [es.enter_context(nc.sbuf_tensor(f"s1_xnf{i}", [128, D], F32)) for i in range(2)]
            xnb = [es.enter_context(nc.sbuf_tensor(f"s1_xnb{i}", [128, D], BF16)) for i in range(2)]
            amq = es.enter_context(nc.sbuf_tensor("s1_amq", [128, 2], F32))
            rq1 = es.enter_context(nc.sbuf_tensor("s1_rq", [128, 2], F32))
            yq1 = [es.enter_context(nc.sbuf_tensor(f"s1_yq{i}", [128, D], F32)) for i in range(2)]
            ti1 = [es.enter_context(nc.sbuf_tensor(f"s1_ti{i}", [128, D], I32)) for i in range(2)]
            qn1 = [es.enter_context(nc.sbuf_tensor(f"s1_qn{i}", [128, D], BF16)) for i in range(2)]
            sg1 = [es.enter_context(nc.sbuf_tensor(f"s1_sg{i}", [128, D], BF16)) for i in range(2)]
            p = Plan()
            LX = [LA, LB]
            TXN = [TA, TB]
            TQX = [TC, TD]
            tA_dve = [0, 0]
            for t in range(NT):
                bi = t % 2
                lw0 = entry() if t == 0 else []
                t_l = p.add("sync", lambda e, t=t, bi=bi: e.dma_start(
                    out=xt[bi][:], in_=x_in[t * 128:(t + 1) * 128, :]),
                    waits=lw0 + ([(sDVE, tA_dve[bi])] if t >= 2 else []),
                    inc=LX[bi], inck=16)
                t_sq = p.add("scalar", lambda e, bi=bi: e.activation(
                    out=x2[bi][:], in_=xt[bi][:], func=AF.Square),
                    waits=[(LX[bi], t_l)], inc=sACT)
                p.add("vector", lambda e, t=t, bi=bi: e.reduce_sum(
                    out=s1s[:, t:t + 1], in_=xt[bi][:], axis=AX.X),
                    waits=[(LX[bi], t_l)], inc=sDVE)
                t_r2 = p.add("vector", lambda e, t=t, bi=bi: e.reduce_sum(
                    out=s2s[:, t:t + 1], in_=x2[bi][:], axis=AX.X),
                    waits=[(sACT, t_sq)], inc=sDVE)
                tA_dve[bi] = t_r2
            t1 = p.add("vector", lambda e: e.tensor_scalar(
                out=negm[:], in0=s1s[:], scalar1=-1.0 / D, scalar2=None, op0=ALU.mult),
                waits=[(sDVE, sDVE.n)], inc=sDVE)
            t2 = p.add("scalar", lambda e: e.activation(
                out=m2[:], in_=s1s[:], func=AF.Square, scale=1.0 / D),
                waits=[(sDVE, t1)], inc=sACT)
            t3 = p.add("vector", lambda e: e.scalar_tensor_tensor(
                out=varb[:], in0=s2s[:], scalar=1.0 / D, in1=m2[:],
                op0=ALU.mult, op1=ALU.subtract), waits=[(sACT, t2)], inc=sDVE)
            t4 = p.add("scalar", lambda e: e.activation(
                out=sdv[:], in_=varb[:], func=AF.Sqrt, bias=eps_c[:]),
                waits=[(sDVE, t3)], inc=sACT)
            t5 = p.add("vector", lambda e: e.reciprocal(out=rstd[:], in_=sdv[:]),
                       waits=[(sACT, t4)], inc=sDVE)
            t6 = p.add("vector", lambda e: e.tensor_mul(out=nmr[:], in0=negm[:], in1=rstd[:]),
                       waits=[(sDVE, t5)], inc=sDVE)
            tB_act = [0, 0]
            tB_st = [0, 0]
            tB_xnst = [0, 0]
            for t in range(NT):
                bi = t % 2
                quant = t < ONT
                t_l = p.add("sync", lambda e, t=t, bi=bi: e.dma_start(
                    out=xt[bi][:], in_=x_in[t * 128:(t + 1) * 128, :]),
                    waits=([(sACT, tB_act[bi])] if t >= 2
                           else [(sDVE, tA_dve[bi])]), inc=LX[bi], inck=16)
                t_xn = p.add("scalar", lambda e, t=t, bi=bi: e.activation(
                    out=xnf[bi][:], in_=xt[bi][:], func=AF.Identity,
                    scale=rstd[:, t:t + 1], bias=nmr[:, t:t + 1]),
                    waits=[(LX[bi], t_l), (sDVE, t6)]
                          + ([(TQX[bi], tB_st[bi])] if 2 <= t < ONT + 2 else [])
                          + ([(TXN[bi], tB_xnst[bi])] if t >= ONT + 2 else []),
                    inc=sACT)
                t_bf = p.add("vector", lambda e, bi=bi: e.tensor_copy(
                    out=xnb[bi][:], in_=xnf[bi][:]),
                    waits=[(sACT, t_xn)] + ([(TXN[bi], tB_xnst[bi])] if t >= 2 else []),
                    inc=sDVE)
                t_xst = p.add("gpsimd", lambda e, t=t, bi=bi: e.dma_start(
                    out=xn_tok[t * 128:(t + 1) * 128, :], in_=xnb[bi][:]),
                    waits=[(sDVE, t_bf)], inc=TXN[bi], inck=16)
                tB_xnst[bi] = t_xst
                if not quant:
                    tB_act[bi] = t_xn
                    continue
                t_am = p.add("vector", lambda e, bi=bi: e.reduce_max(
                    out=amq[:, bi:bi + 1], in_=xnf[bi][:], axis=AX.X,
                    apply_absolute_value=True), waits=[(sACT, t_xn)], inc=sDVE)
                t_s = p.add("vector", lambda e, t=t, bi=bi: e.tensor_scalar(
                    out=sx_all[:, t:t + 1], in0=amq[:, bi:bi + 1], scalar1=INV127,
                    scalar2=EPS_Q, op0=ALU.mult, op1=ALU.max),
                    waits=[(sDVE, t_am)], inc=sDVE)
                t_r = p.add("vector", lambda e, t=t, bi=bi: e.reciprocal(
                    out=rq1[:, bi:bi + 1], in_=sx_all[:, t:t + 1]),
                    waits=[(sDVE, t_s)] + ([(sACT, tB_act[bi])] if t >= 2 else []),
                    inc=sDVE)
                t_sg = p.add("scalar", lambda e, bi=bi: e.activation(
                    out=sg1[bi][:], in_=xnf[bi][:], func=AF.Sign),
                    waits=[(sDVE, t_r)], inc=sACT)
                t_y = p.add("scalar", lambda e, bi=bi: e.activation(
                    out=yq1[bi][:], in_=xnf[bi][:], func=AF.Identity,
                    scale=rq1[:, bi:bi + 1], bias=be_ap),
                    waits=[(sDVE, t_r)], inc=sACT)
                tB_act[bi] = t_y
                t_i = p.add("vector", lambda e, bi=bi: e.scalar_tensor_tensor(
                    out=ti1[bi][:], in0=sg1[bi][:], scalar=al_ap,
                    in1=yq1[bi][:], op0=ALU.mult, op1=ALU.add),
                    waits=[(sACT, t_y)], inc=sDVE)
                t_q = p.add("vector", lambda e, bi=bi: e.tensor_scalar(
                    out=qn1[bi][:], in0=ti1[bi][:], scalar1=-1.0, scalar2=None,
                    op0=ALU.mult),
                    waits=[(sDVE, t_i)] + ([(TQX[bi], tB_st[bi])] if t >= 2 else []),
                    inc=sDVE)
                t_st = p.add("gpsimd", lambda e, t=t, bi=bi: e.dma_start(
                    out=qx_st[t * 128:(t + 1) * 128, :], in_=qn1[bi][:]),
                    waits=[(sDVE, t_q)], inc=TQX[bi], inck=16)
                tB_st[bi] = t_st
            p.emit(nc)

        with contextlib.ExitStack() as es:
            xnT = es.enter_context(nc.sbuf_tensor("s2_xnT", [128, 16 * S], BF16, is_transpose_dest=True))
            wqf = [es.enter_context(nc.sbuf_tensor(f"s2_wqf{i}", [128, 16 * 128], F32)) for i in range(2)]
            wqb = [es.enter_context(nc.sbuf_tensor(f"s2_wqb{i}", [128, 16 * 128], BF16)) for i in range(2)]
            kq = [es.enter_context(nc.sbuf_tensor(f"s2_kq{i}", [128, 512], BF16)) for i in range(4)]
            ps2 = [es.enter_context(nc.psum_tensor(f"s2_ps{i}", [128, 512], F32)) for i in range(4)]
            p = Plan()
            LWQ = [LD, LE]
            TKQ = [TA, TB, TC, TD]
            for dc in range(16):
                for tb in range(8):
                    p.add("sync", lambda e, dc=dc, tb=tb: e.dma_start(
                        out=xnT[:, dc * S + tb * 512: dc * S + (tb + 1) * 512],
                        in_=xn_tok[tb * 512:(tb + 1) * 512, dc * 128:(dc + 1) * 128],
                        transpose=True),
                        waits=(entry() if (dc == 0 and tb == 0) else []),
                        inc=LC, inck=16)
            t_xnT = LC.n
            tw_cast = [0, 0]
            pe_oc = [0, 0]
            tk_evac = {}
            tk_st = {}
            unit = 0
            wqkv_r = wqkv.rearrange("(dc p) o -> p dc o", p=128)
            for oc in range(48):
                bi = oc % 2
                t_wl = p.add("sync", lambda e, oc=oc, bi=bi: e.dma_start(
                    out=wqf[bi][:],
                    in_=wqkv_r[:, :, oc * 128:(oc + 1) * 128]),
                    waits=([(sDVE, tw_cast[bi])] if oc >= 2 else []),
                    inc=LWQ[bi], inck=16)
                t_c = p.add("vector", lambda e, bi=bi: e.tensor_copy(
                    out=wqb[bi][:], in_=wqf[bi][:]),
                    waits=[(LWQ[bi], t_wl)] + ([(sPE, pe_oc[bi])] if oc >= 2 else []),
                    inc=sDVE)
                tw_cast[bi] = t_c
                qsc = QSC if oc < 16 else 1.0
                for tb in range(4 if oc < 16 else 8):
                    bank = unit % 4
                    mm_w = [(sDVE, t_c), (LC, t_xnT)]
                    if unit >= 4:
                        mm_w.append((sACT, tk_evac[unit - 4]))
                    for dc in range(16):
                        t_mm = p.add("tensor", lambda e, bank=bank, bi=bi, dc=dc, tb=tb: e.matmul(
                            ps2[bank][:], wqb[bi][:, dc * 128:(dc + 1) * 128],
                            xnT[:, dc * S + tb * 512: dc * S + (tb + 1) * 512],
                            start=(dc == 0), stop=(dc == 15)),
                            waits=(mm_w if dc == 0 else []),
                            inc=(sPE if dc == 15 else None))
                    t_ev = p.add("scalar", lambda e, bank=bank, oc=oc, qsc=qsc: e.activation(
                        out=kq[bank][:], in_=ps2[bank][:], func=AF.Identity,
                        scale=qsc, bias=bqs_sb[:, oc:oc + 1]),
                        waits=[(sPE, t_mm)] + ([(TKQ[bank], tk_st[unit - 4])] if unit >= 4 else []),
                        inc=sACT)
                    tk_evac[unit] = t_ev
                    t_st = p.add("gpsimd", lambda e, bank=bank, oc=oc, tb=tb: e.dma_start(
                        out=kqvT[oc * 128:(oc + 1) * 128, tb * 512:(tb + 1) * 512],
                        in_=kq[bank][:]), waits=[(sACT, t_ev)], inc=TKQ[bank], inck=16)
                    tk_st[unit] = t_st
                    pe_oc[bi] = t_mm
                    unit += 1
            p.emit(nc)

        with contextlib.ExitStack() as es:
            QT = [es.enter_context(nc.sbuf_tensor(f"s3_QT{i}", [128, OS], BF16)) for i in range(2)]
            KT = [es.enter_context(nc.sbuf_tensor(f"s3_KT{i}", [128, S], BF16)) for i in range(2)]
            VT = [es.enter_context(nc.sbuf_tensor(f"s3_VT{i}", [128, S], BF16, is_transpose_dest=True)) for i in range(2)]
            ptall = es.enter_context(nc.sbuf_tensor("s3_pt", [128, 32 * 512], BF16))
            rbs = [es.enter_context(nc.sbuf_tensor(f"s3_rbs{i}", [128, 512], F32)) for i in range(2)]
            otb = [es.enter_context(nc.sbuf_tensor(f"s3_otb{i}", [128, 512], BF16)) for i in range(2)]
            ssp = [es.enter_context(nc.psum_tensor(f"s3_ssp{i}", [128, 1024], F32)) for i in range(2)]
            pot = [es.enter_context(nc.psum_tensor(f"s3_ot{i}", [128, 512], F32)) for i in range(2)]
            pdn = [es.enter_context(nc.psum_tensor(f"s3_dn{i}", [128, 512], F32)) for i in range(2)]
            CW = 1408
            wt = [es.enter_context(nc.sbuf_tensor(f"s0_wt{i}", [128, DFF], F32)) for i in range(2)]
            qn0 = [es.enter_context(nc.sbuf_tensor(f"s0_qn{i}", [128, DFF], BF16)) for i in range(2)]
            yq0 = [es.enter_context(nc.sbuf_tensor(f"s0_yq{i}", [128, CW], F32)) for i in range(4)]
            sg0 = [es.enter_context(nc.sbuf_tensor(f"s0_sg{i}", [128, CW], BF16)) for i in range(4)]
            ti0 = [es.enter_context(nc.sbuf_tensor(f"s0_ti{i}", [128, CW], I32)) for i in range(2)]
            am_all = es.enter_context(nc.sbuf_tensor("s0_am", [128, 104], F32))
            sc_all = es.enter_context(nc.sbuf_tensor("s0_sc", [128, 104], F32))
            rc_all = es.enter_context(nc.sbuf_tensor("s0_rc", [128, 104], F32))
            p = Plan()
            LH = [LA, LB]
            TOT = [TA, TB]

            NR1 = 2 * DFF // 128
            NR2 = D // 128
            NR = NR1 + NR2
            rowinfo = []
            for r in range(NR1):
                rowinfo.append((w1_in, w1q_st, s1d_t, r, D, 512))
            for r in range(NR2):
                rowinfo.append((w2_in, w2q_st, s2d_t, r, DFF, 1408))
            LW = [LI, LJ]
            TQ = [TC, TD]
            s0_load = {}
            s0_rc = {}
            s0_am = {}
            s0_ylast = {}
            s0_y = {}
            s0_stt = {}
            s0_neg_last = {}
            s0_store = {}

            s0_fifo = []
            s0_glast = {}
            s0_g = {}

            def s0_A(r):
                w_ap, _, _, rr, ncol, cw = rowinfo[r]
                bi = r % 2

                def go(w_ap=w_ap, rr=rr, bi=bi, ncol=ncol, r=r):
                    lw = []
                    if r >= 2:
                        lw.append((sACT, s0_ylast[r - 2]))
                        lw.append((sDVE, s0_glast[r - 2]))
                    s0_load[r] = p.add("gpsimd", lambda e: e.dma_start(
                        out=wt[bi][:, 0:ncol], in_=w_ap[rr * 128:(rr + 1) * 128, :]),
                        waits=lw, inc=LW[bi], inck=16)
                s0_fifo.append(go)

            def s0_B(r):
                _, _, sd_t, rr, ncol, cw = rowinfo[r]
                bi = r % 2

                def go(sd_t=sd_t, rr=rr, ncol=ncol, bi=bi, r=r):
                    t_am = p.add("vector", lambda e: e.reduce_max(
                        out=am_all[:, r:r + 1], in_=wt[bi][:, 0:ncol], axis=AX.X,
                        apply_absolute_value=True), waits=[(LW[bi], s0_load[r])], inc=sDVE)
                    s0_am[r] = t_am
                    t_s = p.add("vector", lambda e: e.tensor_scalar(
                        out=sc_all[:, r:r + 1], in0=am_all[:, r:r + 1], scalar1=INV127,
                        scalar2=EPS_Q, op0=ALU.mult, op1=ALU.max),
                        waits=[(sDVE, t_am)], inc=sDVE)
                    t_rc = p.add("vector", lambda e: e.reciprocal(
                        out=rc_all[:, r:r + 1], in_=sc_all[:, r:r + 1]),
                        waits=[(sDVE, t_s)], inc=sDVE)
                    s0_rc[r] = t_rc
                    p.add("gpsimd", lambda e: e.dma_start(
                        out=bass.AP(sd_t, rr * 128, [[1, 128]]),
                        in_=sc_all[:, r:r + 1]),
                        waits=[(sDVE, t_s)], inc=TE, inck=16)
                s0_fifo.append(go)

            def s0_C(r):
                _, _, _, rr, ncol, cw = rowinfo[r]
                bi = r % 2
                nch = ncol // cw
                s0_y[r] = {}
                for c in range(nch):
                    sl = slice(c * cw, (c + 1) * cw)

                    def go(bi=bi, sl=sl, c=c, r=r, nch=nch):
                        sgw = [(LW[bi], s0_load[r])]
                        if r >= 1:
                            sgw.append((sDVE, s0_stt[r - 1][c]))
                        if c == 0:
                            sgw.append((sDVE, s0_rc[r]))
                        t_sg = p.add("vector", lambda e: e.tensor_scalar(
                            out=sg0[c][:, 0:sl.stop - sl.start], in0=wt[bi][:, sl],
                            scalar1=0.0, scalar2=None, op0=ALU.is_gt),
                            waits=sgw, inc=sDVE)
                        s0_glast[r] = t_sg
                        s0_g.setdefault(r, {})[c] = t_sg
                        t_y = p.add("scalar", lambda e: e.activation(
                            out=yq0[c][:, 0:sl.stop - sl.start], in_=wt[bi][:, sl],
                            func=AF.Identity, scale=rc_all[:, r:r + 1], bias=dm_ap),
                            waits=([(sDVE, s0_rc[r])] if c == 0 else []) +
                                  ([(sDVE, s0_stt[r - 1][c])] if r >= 1 else []),
                            inc=sACT)
                        s0_y[r][c] = t_y
                        if c == nch - 1:
                            s0_ylast[r] = t_y
                    s0_fifo.append(go)

            s0_ti_user = {}

            def s0_D(r):
                _, wq_ap, _, rr, ncol, cw = rowinfo[r]
                bi = r % 2
                nch = ncol // cw
                s0_stt[r] = {}
                for c in range(nch):
                    sl = slice(c * cw, (c + 1) * cw)

                    def go(bi=bi, sl=sl, c=c, r=r, nch=nch, wq_ap=wq_ap,
                           rr=rr, ncol=ncol):
                        iw = [(sACT, s0_y[r][c]), (sDVE, s0_g[r][c])]
                        if c % 2 in s0_ti_user:
                            iw.append((sACT, s0_ti_user[c % 2]))
                        t_i = p.add("vector", lambda e: e.scalar_tensor_tensor(
                            out=ti0[c % 2][:, 0:sl.stop - sl.start],
                            in0=sg0[c][:, 0:sl.stop - sl.start], scalar=ga_ap,
                            in1=yq0[c][:, 0:sl.stop - sl.start], op0=ALU.mult, op1=ALU.add),
                            waits=iw, inc=sDVE)
                        s0_stt[r][c] = t_i
                        nw = [(sDVE, t_i)]
                        if c == 0 and r >= 2:
                            nw.append((TQ[bi], s0_store[r - 2]))
                        t_neg = p.add("scalar", lambda e: e.activation(
                            out=qn0[bi][:, sl], in_=ti0[c % 2][:, 0:sl.stop - sl.start],
                            func=AF.Identity, scale=-1.0),
                            waits=nw, inc=sACT)
                        s0_ti_user[c % 2] = t_neg
                        if c == nch - 1:
                            s0_neg_last[r] = t_neg
                            s0_store[r] = p.add("gpsimd", lambda e: e.dma_start(
                                out=wq_ap[rr * 128:(rr + 1) * 128, :],
                                in_=qn0[bi][:, 0:ncol]),
                                waits=[(sACT, t_neg)], inc=TQ[bi], inck=16)
                    s0_fifo.append(go)

            s0_cursors = [0, 0, 0, 0]
            while s0_cursors != [NR, NR, NR, NR]:
                cur = s0_cursors
                before = list(cur)
                t = min(NR, cur[2])
                while cur[3] < t:
                    s0_D(cur[3]); cur[3] += 1
                t = min(NR, cur[0])
                while cur[1] < t:
                    s0_B(cur[1]); cur[1] += 1
                t = min(NR, cur[0], cur[1], cur[3] + 1)
                while cur[2] < t:
                    s0_C(cur[2]); cur[2] += 1
                t = min(NR, cur[2] + 2, cur[1] + 2)
                while cur[0] < t:
                    s0_A(cur[0]); cur[0] += 1
                if cur == before:
                    raise RuntimeError(f"s0 enqueue stuck at {cur}")
            s0_pos = [0]

            def s0_pump(k):
                n_ = min(len(s0_fifo) - s0_pos[0], k)
                for _ in range(n_):
                    s0_fifo[s0_pos[0]]()
                    s0_pos[0] += 1

            def s0_flush():
                s0_pump(len(s0_fifo))
            h_done = [0, 0]
            ld_tok = [0, 0]
            prev_exp_last = 0
            d_prev = 0
            ot_evac_pb = [0, 0]
            rc_pb = [0, 0]
            st_tok3 = [0, 0]
            first_qb = True
            for n in range(NH):
                bi = n % 2
                lw = ([(sPE, h_done[bi])] if n >= 2
                      else (entry() if n == 0 else []))
                p.add("sync", lambda e, n=n, bi=bi: e.dma_start(
                    out=QT[bi][:], in_=kqvT[n * 128:(n + 1) * 128, 0:OS]),
                    waits=lw, inc=LH[bi], inck=16)
                p.add("sync", lambda e, n=n, bi=bi: e.dma_start(
                    out=KT[bi][:], in_=kqvT[D + n * 128: D + (n + 1) * 128, :]),
                    inc=LH[bi], inck=16)
                for kc in range(32):
                    p.add("sync", lambda e, n=n, bi=bi, kc=kc: e.dma_start(
                        out=VT[bi][:, kc * 128:(kc + 1) * 128],
                        in_=kqvT[2 * D + n * 128: 2 * D + (n + 1) * 128,
                                 kc * 128:(kc + 1) * 128], transpose=True),
                        inc=LH[bi], inck=16)
                ld_tok[bi] = LH[bi].n

                for qb in range(4):
                    nown = 4 * (qb + 1)
                    nk = 2 * nown
                    npair = nk // 2
                    pb = (n * 4 + qb) % 2

                    def kaddr(kc, nown=nown):
                        if kc < nown:
                            return kc * 128
                        return OS + (kc - nown) * 128

                    exp_tok = {}
                    mask_tok = {}

                    def emit_avd(kc):
                        jp = kc // 2
                        avw = [(sACT, exp_tok[jp])]
                        if jp in mask_tok:
                            avw.append((sDVE, mask_tok[jp]))
                        if kc == 0 and ot_evac_pb[pb]:
                            avw.append((sDVE, ot_evac_pb[pb]))
                        t = p.add("tensor", lambda e, kc=kc, nk=nk, bi=bi, pb=pb, ka=kaddr(kc): e.matmul(
                            pot[pb][:], VT[bi][:, ka:ka + 128],
                            ptall[:, kc * 512:(kc + 1) * 512],
                            start=(kc == 0), stop=(kc == nk - 1)),
                            waits=avw, inc=sPE)
                        dw = []
                        if kc == 0 and rc_pb[pb]:
                            dw.append((sDVE, rc_pb[pb]))
                        t = p.add("tensor", lambda e, kc=kc, nk=nk, pb=pb: e.matmul(
                            pdn[pb][:], ones_bb[:],
                            ptall[:, kc * 512:(kc + 1) * 512],
                            start=(kc == 0), stop=(kc == nk - 1)),
                            waits=dw, inc=sPE)
                        return t

                    t_d = 0
                    for jp in range(npair):
                        kc0 = 2 * jp
                        sb_ = jp % 2
                        ssw = [(LH[bi], ld_tok[bi])]
                        if jp >= 2:
                            ssw.append((sACT, exp_tok[jp - 2]))
                        elif not first_qb:
                            ssw.append((sACT, prev_exp_last))
                        if jp == 0 and not first_qb:
                            ssw.append((sPE, d_prev))
                        p.add("tensor", lambda e, sb_=sb_, qb=qb, bi=bi, ka=kaddr(kc0): e.matmul(
                            ssp[sb_][:, 0:512], KT[bi][:, ka:ka + 128],
                            QT[bi][:, qb * 512:(qb + 1) * 512], start=True, stop=True),
                            waits=ssw, inc=sPE)
                        t_ss = p.add("tensor", lambda e, sb_=sb_, qb=qb, bi=bi, ka=kaddr(kc0 + 1): e.matmul(
                            ssp[sb_][:, 512:1024], KT[bi][:, ka:ka + 128],
                            QT[bi][:, qb * 512:(qb + 1) * 512], start=True, stop=True),
                            inc=sPE)
                        t_e = p.add("scalar", lambda e, sb_=sb_, kc0=kc0: e.activation(
                            out=ptall[:, kc0 * 512:(kc0 + 2) * 512], in_=ssp[sb_][:],
                            func=AF.Exp), waits=[(sPE, t_ss)], inc=sACT)
                        exp_tok[jp] = t_e
                        d_ = None
                        if 2 * qb <= jp < 2 * qb + 2:
                            d_ = 2 * (jp - 2 * qb)
                        elif jp >= npair - 2:
                            d_ = 4
                        if d_ is not None:
                            t_m = p.add("vector", lambda e, kc0=kc0, d_=d_: e.tensor_mul(
                                out=ptall[:, kc0 * 512:(kc0 + 2) * 512],
                                in0=ptall[:, kc0 * 512:(kc0 + 2) * 512],
                                in1=masks_sb[:, d_ * 512:(d_ + 2) * 512]),
                                waits=[(sACT, t_e)], inc=sDVE)
                            mask_tok[jp] = t_m
                        if jp >= 1:
                            emit_avd(kc0 - 2)
                            emit_avd(kc0 - 1)
                        s0_pump(4)
                    emit_avd(nk - 2)
                    t_d = emit_avd(nk - 1)
                    d_prev = t_d
                    prev_exp_last = exp_tok[npair - 1]
                    first_qb = False
                    t_rc = p.add("vector", lambda e, pb=pb: e.reciprocal(
                        out=rbs[pb][:], in_=pdn[pb][:]),
                        waits=[(sPE, t_d)], inc=sDVE)
                    rc_pb[pb] = t_rc
                    ob = pb
                    ow = [(sDVE, t_rc)]
                    if n * 8 + qb >= 2:
                        ow.append((TOT[ob], st_tok3[ob]))
                    t_ob = p.add("vector", lambda e, ob=ob, pb=pb: e.tensor_mul(
                        out=otb[ob][:], in0=pot[pb][:], in1=rbs[pb][:]), waits=ow, inc=sDVE)
                    ot_evac_pb[pb] = t_ob
                    t_so = p.add("gpsimd", lambda e, n=n, qb=qb, ob=ob: e.dma_start(
                        out=otT[n * 128:(n + 1) * 128, qb * 512:(qb + 1) * 512],
                        in_=otb[ob][:]), waits=[(sDVE, t_ob)], inc=TOT[ob], inck=16)
                    st_tok3[ob] = t_so
                h_done[bi] = sPE.n
            s0_flush()
            p.emit(nc)

        with contextlib.ExitStack() as es:
            qxT = es.enter_context(nc.sbuf_tensor("s5_qxT", [128, 16 * OS], BF16, is_transpose_dest=True))
            w1g = [es.enter_context(nc.sbuf_tensor(f"s5_w1g{i}", [128, 16 * 512], BF16, is_transpose_dest=True)) for i in range(2)]
            w1u = [es.enter_context(nc.sbuf_tensor(f"s5_w1u{i}", [128, 16 * 512], BF16, is_transpose_dest=True)) for i in range(2)]
            sb1g = [es.enter_context(nc.sbuf_tensor(f"s5_sb1g{i}", [128, 512], F32)) for i in range(2)]
            sb1u = [es.enter_context(nc.sbuf_tensor(f"s5_sb1u{i}", [128, 512], F32)) for i in range(2)]
            gd = [es.enter_context(nc.sbuf_tensor(f"s5_gd{i}", [128, 512], F32)) for i in range(2)]
            sgt = [es.enter_context(nc.sbuf_tensor(f"s5_sg{i}", [128, 512], F32)) for i in range(2)]
            ud = [es.enter_context(nc.sbuf_tensor(f"s5_ud{i}", [128, 512], F32)) for i in range(2)]
            hb = [es.enter_context(nc.sbuf_tensor(f"s5_hb{i}", [128, 512], BF16)) for i in range(2)]
            pg = [es.enter_context(nc.psum_tensor(f"s5_pg{i}", [128, 512], F32)) for i in range(2)]
            pu = [es.enter_context(nc.psum_tensor(f"s5_pu{i}", [128, 512], F32)) for i in range(2)]
            p = Plan()
            THB = [TA, TB]
            LBF = [LB, LF]
            for hc in range(16):
                for tbg in range(2):
                    p.add("sync", lambda e, hc=hc, tbg=tbg: e.dma_start(
                        out=qxT[:, hc * OS + tbg * 1024: hc * OS + (tbg + 1) * 1024],
                        in_=qx_st[tbg * 1024:(tbg + 1) * 1024, hc * 128:(hc + 1) * 128],
                        transpose=True),
                        waits=(entry() if (hc == 0 and tbg == 0) else []),
                        inc=LA, inck=16)
            t_qxT = LA.n
            ev_h = {}
            st_h = {}
            pe_blk = {}
            ev_blk = {}
            t_wl = {}
            unit = 0

            def emit_w1_batch(ib):
                wb = ib % 2
                ww = ([(sPE, pe_blk[ib - 2]), (sDVE, ev_blk[ib - 2])]
                      if ib >= 2 else [])
                for hc in range(16):
                    p.add("sync", lambda e, ib=ib, hc=hc, wb=wb: e.dma_start(
                        out=w1g[wb][:, hc * 512:(hc + 1) * 512],
                        in_=w1q_st[ib * 512:(ib + 1) * 512, hc * 128:(hc + 1) * 128],
                        transpose=True), waits=(ww if hc == 0 else []), inc=LBF[wb], inck=16)
                    p.add("sync", lambda e, ib=ib, hc=hc, wb=wb: e.dma_start(
                        out=w1u[wb][:, hc * 512:(hc + 1) * 512],
                        in_=w1q_st[DFF + ib * 512: DFF + (ib + 1) * 512,
                                   hc * 128:(hc + 1) * 128],
                        transpose=True), inc=LBF[wb], inck=16)
                p.add("sync", lambda e, ib=ib, wb=wb: e.dma_start(
                    out=sb1g[wb][:], in_=bass.AP(s1d_t, ib * 512, [[0, 128], [1, 512]])),
                    inc=LBF[wb], inck=16)
                t_wl[ib] = p.add("sync", lambda e, ib=ib, wb=wb: e.dma_start(
                    out=sb1u[wb][:], in_=bass.AP(s1d_t, DFF + ib * 512, [[0, 128], [1, 512]])),
                    inc=LBF[wb], inck=16)

            emit_w1_batch(0)
            for ib in range(NIB):
                if ib + 1 < NIB:
                    emit_w1_batch(ib + 1)
                wb = ib % 2
                for tb in range(ONT):
                    gi = unit % 2
                    mwg = [(LBF[wb], t_wl[ib]), (LA, t_qxT)]
                    if unit >= 2:
                        mwg.append((sDVE, ev_h[unit - 2][0]))
                    for hc in range(16):
                        t_mg = p.add("tensor", lambda e, gi=gi, hc=hc, tb=tb, wb=wb: e.matmul(
                            pg[gi][:], qxT[:, hc * OS + tb * 128: hc * OS + (tb + 1) * 128],
                            w1g[wb][:, hc * 512:(hc + 1) * 512],
                            start=(hc == 0), stop=(hc == 15)),
                            waits=(mwg if hc == 0 else []), inc=(sPE if hc == 15 else None))
                    mwu = []
                    if unit >= 2:
                        mwu.append((sDVE, ev_h[unit - 2][1]))
                    for hc in range(16):
                        t_mu = p.add("tensor", lambda e, gi=gi, hc=hc, tb=tb, wb=wb: e.matmul(
                            pu[gi][:], qxT[:, hc * OS + tb * 128: hc * OS + (tb + 1) * 128],
                            w1u[wb][:, hc * 512:(hc + 1) * 512],
                            start=(hc == 0), stop=(hc == 15)),
                            waits=(mwu if hc == 0 else []), inc=(sPE if hc == 15 else None))
                    t_gd = p.add("vector", lambda e, gi=gi, tb=tb, wb=wb: e.scalar_tensor_tensor(
                        out=gd[gi][:], in0=pg[gi][:], scalar=sx_all[:, tb:tb + 1],
                        in1=sb1g[wb][:], op0=ALU.mult, op1=ALU.mult),
                        waits=[(sPE, t_mg)] + ([(sACT, ev_h[unit - 2][2])] if unit >= 2 else []),
                        inc=sDVE)
                    t_sg = p.add("scalar", lambda e, gi=gi: e.activation(
                        out=sgt[gi][:], in_=gd[gi][:], func=AF.Silu),
                        waits=[(sDVE, t_gd)], inc=sACT)
                    t_ud = p.add("vector", lambda e, gi=gi, tb=tb, wb=wb: e.scalar_tensor_tensor(
                        out=ud[gi][:], in0=pu[gi][:], scalar=sx_all[:, tb:tb + 1],
                        in1=sb1u[wb][:], op0=ALU.mult, op1=ALU.mult),
                        waits=[(sPE, t_mu)], inc=sDVE)
                    hw = [(sACT, t_sg), (sDVE, t_ud)]
                    if unit >= 2:
                        hw.append((THB[gi], st_h[unit - 2]))
                    t_h = p.add("vector", lambda e, gi=gi: e.tensor_mul(
                        out=hb[gi][:], in0=sgt[gi][:], in1=ud[gi][:]),
                        waits=hw, inc=sDVE)
                    t_hs = p.add("gpsimd", lambda e, gi=gi, ib=ib, tb=tb: e.dma_start(
                        out=h_st[tb * 128:(tb + 1) * 128, ib * 512:(ib + 1) * 512],
                        in_=hb[gi][:]), waits=[(sDVE, t_h)], inc=THB[gi], inck=16)
                    st_h[unit] = t_hs
                    ev_h[unit] = (t_gd, t_ud, t_sg)
                    unit += 1
                pe_blk[ib] = sPE.n
                ev_blk[ib] = ev_h[unit - 1][1]
            p.emit(nc)

        with contextlib.ExitStack() as es:
            wof = es.enter_context(nc.sbuf_tensor("s4_wof", [128, D], F32))
            wob = es.enter_context(nc.sbuf_tensor("s4_wob", [128, 16 * D], BF16))
            bob = es.enter_context(nc.sbuf_tensor("s4_bob", [128, D], F32))
            otc = [es.enter_context(nc.sbuf_tensor(f"s4_otc{i}", [128, 16 * 128], BF16)) for i in range(2)]
            af = [es.enter_context(nc.sbuf_tensor(f"s4_af{i}", [128, 512], F32)) for i in range(4)]
            ps4 = [es.enter_context(nc.psum_tensor(f"s4_ps{i}", [128, 512], F32)) for i in range(4)]
            BCW = 1408
            hrow = [es.enter_context(nc.sbuf_tensor(f"s5b_h{i}", [128, DFF], BF16)) for i in range(2)]
            qnh = [es.enter_context(nc.sbuf_tensor(f"s5b_qn{i}", [128, DFF], BF16)) for i in range(2)]
            yqh4 = [es.enter_context(nc.sbuf_tensor(f"s5b_yq{i}", [128, BCW], F32)) for i in range(4)]
            sgh4 = [es.enter_context(nc.sbuf_tensor(f"s5b_sg{i}", [128, BCW], BF16)) for i in range(4)]
            tih2 = [es.enter_context(nc.sbuf_tensor(f"s5b_ti{i}", [128, BCW], I32)) for i in range(2)]
            amh = es.enter_context(nc.sbuf_tensor("s5b_am", [128, 2], F32))
            rqh = es.enter_context(nc.sbuf_tensor("s5b_rq", [128, 2], F32))
            p = Plan()
            LOT = [LC, LD]
            TAF = [TA, TB, TC, TD]
            LHR = [LB, LF]
            TNH = [TE, TF]
            p.add("sync", lambda e: e.dma_start(
                out=bob[:], in_=bass.AP(bo_t, 0, [[0, 128], [1, D]])),
                waits=entry(), inc=LE, inck=16)
            t_wc = 0
            for hc in range(16):
                t_wl = p.add("sync", lambda e, hc=hc: e.dma_start(
                    out=wof[:], in_=wo_in[hc * 128:(hc + 1) * 128, :]),
                    waits=([(sDVE, t_wc)] if hc > 0 else []), inc=LE, inck=16)
                t_wc = p.add("vector", lambda e, hc=hc: e.tensor_copy(
                    out=wob[:, hc * D:(hc + 1) * D], in_=wof[:]),
                    waits=[(LE, t_wl)], inc=sDVE)
            otT_r = otT.rearrange("(hc p) q -> p hc q", p=128)
            ev4 = {}
            st4 = {}
            pe_tok4 = 0
            unit = 0
            bq_ylast = {}
            bq_am = {}
            bq_stt = {}
            bq_neg = {}
            bq_store = {}
            for tt in range(ONT):
                bi = tt % 2
                t_l = p.add("sync", lambda e, tt=tt, bi=bi: e.dma_start(
                    out=otc[bi][:], in_=otT_r[:, :, tt * 128:(tt + 1) * 128]),
                    waits=([(sPE, pe_tok4)] if tt >= 2 else []), inc=LOT[bi], inck=16)
                for db in range(4):
                    bank = unit % 4
                    mw = [(LOT[bi], t_l), (sDVE, t_wc)]
                    if unit >= 4:
                        mw.append((sDVE, ev4[unit - 4]))
                    for hc in range(16):
                        t_mm = p.add("tensor", lambda e, bank=bank, bi=bi, hc=hc, db=db: e.matmul(
                            ps4[bank][:], otc[bi][:, hc * 128:(hc + 1) * 128],
                            wob[:, hc * D + db * 512: hc * D + (db + 1) * 512],
                            start=(hc == 0), stop=(hc == 15)),
                            waits=(mw if hc == 0 else []), inc=(sPE if hc == 15 else None))
                    pe_tok4 = t_mm
                    ew = [(sPE, t_mm)]
                    if unit >= 4:
                        ew.append((TAF[bank], st4[unit - 4]))
                    t_ev = p.add("vector", lambda e, bank=bank, db=db: e.tensor_tensor(
                        out=af[bank][:], in0=ps4[bank][:],
                        in1=bob[:, db * 512:(db + 1) * 512], op=ALU.add),
                        waits=ew, inc=sDVE)
                    ev4[unit] = t_ev
                    t_st = p.add("gpsimd", lambda e, bank=bank, tt=tt, db=db: e.dma_start(
                        out=abuf[tt * 128:(tt + 1) * 128, db * 512:(db + 1) * 512],
                        in_=af[bank][:]), waits=[(sDVE, t_ev)], inc=TAF[bank], inck=16)
                    st4[unit] = t_st
                    unit += 1
                hlw = []
                if tt >= 2:
                    hlw = [(sACT, bq_ylast[tt - 2]), (sDVE, bq_am[tt - 2])]
                t_hl = p.add("sync", lambda e, tt=tt, bi=bi: e.dma_start(
                    out=hrow[bi][:], in_=h_st[tt * 128:(tt + 1) * 128, :]),
                    waits=hlw, inc=LHR[bi], inck=16)
                t_am = p.add("vector", lambda e, bi=bi, tt=tt: e.reduce_max(
                    out=amh[:, bi:bi + 1], in_=hrow[bi][:], axis=AX.X,
                    apply_absolute_value=True), waits=[(LHR[bi], t_hl)], inc=sDVE)
                bq_am[tt] = t_am
                t_s = p.add("vector", lambda e, tt=tt, bi=bi: e.tensor_scalar(
                    out=sh_all[:, tt:tt + 1], in0=amh[:, bi:bi + 1], scalar1=INV127,
                    scalar2=EPS_Q, op0=ALU.mult, op1=ALU.max),
                    waits=[(sDVE, t_am)], inc=sDVE)
                t_r = p.add("vector", lambda e, tt=tt, bi=bi: e.reciprocal(
                    out=rqh[:, bi:bi + 1], in_=sh_all[:, tt:tt + 1]),
                    waits=[(sDVE, t_s)], inc=sDVE)
                t_neg = 0
                for c in range(4):
                    sl = slice(c * BCW, (c + 1) * BCW)
                    sgw = [(LHR[bi], t_hl)]
                    if tt >= 1:
                        sgw.append((sDVE, bq_stt[(tt - 1, c)]))
                    t_sg = p.add("scalar", lambda e, bi=bi, sl=sl, c=c: e.activation(
                        out=sgh4[c][:], in_=hrow[bi][:, sl], func=AF.Sign),
                        waits=sgw, inc=sACT)
                    t_y = p.add("scalar", lambda e, bi=bi, sl=sl, c=c: e.activation(
                        out=yqh4[c][:], in_=hrow[bi][:, sl], func=AF.Identity,
                        scale=rqh[:, bi:bi + 1], bias=be_ap),
                        waits=([(sDVE, t_r)] if c == 0 else []), inc=sACT)
                    bq_ylast[tt] = t_y
                    iw = [(sACT, t_y)]
                    if (tt, c - 2) in bq_neg or (tt - 1, c + 2) in bq_neg:
                        prev = bq_neg.get((tt, c - 2), bq_neg.get((tt - 1, c + 2)))
                        iw.append((sACT, prev))
                    t_i = p.add("vector", lambda e, c=c: e.scalar_tensor_tensor(
                        out=tih2[c % 2][:], in0=sgh4[c][:], scalar=al_ap,
                        in1=yqh4[c][:], op0=ALU.mult, op1=ALU.add),
                        waits=iw, inc=sDVE)
                    bq_stt[(tt, c)] = t_i
                    nw = [(sDVE, t_i)]
                    if c == 0 and tt >= 2:
                        nw.append((TNH[bi], bq_store[tt - 2]))
                    t_neg = p.add("scalar", lambda e, bi=bi, sl=sl, c=c: e.activation(
                        out=qnh[bi][:, sl], in_=tih2[c % 2][:],
                        func=AF.Identity, scale=-1.0),
                        waits=nw, inc=sACT)
                    bq_neg[(tt, c)] = t_neg
                bq_store[tt] = p.add("gpsimd", lambda e, tt=tt, bi=bi: e.dma_start(
                    out=qh_st[tt * 128:(tt + 1) * 128, :], in_=qnh[bi][:]),
                    waits=[(sACT, t_neg)], inc=TNH[bi], inck=16)
            p.emit(nc)

        with contextlib.ExitStack() as es:
            qhT = es.enter_context(nc.sbuf_tensor("s5c_qhT", [128, NIC * 1024], BF16, is_transpose_dest=True))
            w2s = [es.enter_context(nc.sbuf_tensor(f"s5c_w2s{i}", [128, NIC * 512], BF16, is_transpose_dest=True)) for i in range(2)]
            sb2 = [es.enter_context(nc.sbuf_tensor(f"s5c_sb2{i}", [128, 512], F32)) for i in range(2)]
            fd = [es.enter_context(nc.sbuf_tensor(f"s5c_fd{i}", [128, 512], F32)) for i in range(2)]
            xa = [es.enter_context(nc.sbuf_tensor(f"s5c_xa{i}", [128, 512], F32)) for i in range(2)]
            ab = [es.enter_context(nc.sbuf_tensor(f"s5c_ab{i}", [128, 512], F32)) for i in range(2)]
            t1b = [es.enter_context(nc.sbuf_tensor(f"s5c_t1{i}", [128, 512], F32)) for i in range(2)]
            ob5 = [es.enter_context(nc.sbuf_tensor(f"s5c_ob{i}", [128, 512], F32)) for i in range(2)]
            pd5 = [es.enter_context(nc.psum_tensor(f"s5c_pd{i}", [128, 512], F32)) for i in range(2)]
            p = Plan()
            LW2 = [LF, LB]
            LXA = [LC, LD]
            TOU = [TE, TF]
            n_out_units = 0
            pe_sb_end = 0
            pe_db = [0, 0]
            db_dve = [0, 0]
            ev_d = {}
            ld_x = {}
            out_st = {}
            for tbg in range(2):
                t0 = tbg * 1024
                qw5 = [(sPE, pe_sb_end)] if tbg > 0 else entry()
                for ic in range(NIC):
                    p.add("sync", lambda e, t0=t0, ic=ic: e.dma_start(
                        out=qhT[:, ic * 1024:(ic + 1) * 1024],
                        in_=qh_st[t0:t0 + 1024, ic * 128:(ic + 1) * 128], transpose=True),
                        waits=(qw5 if ic == 0 else []), inc=LE, inck=16)
                t_qhT = LE.n
                for db in range(4):
                    bi = db % 2
                    ww = [(sPE, pe_db[bi]), (sDVE, db_dve[bi])] if (tbg * 4 + db) >= 2 else []
                    for ic in range(NIC):
                        p.add("sync", lambda e, db=db, ic=ic, bi=bi: e.dma_start(
                            out=w2s[bi][:, ic * 512:(ic + 1) * 512],
                            in_=w2q_st[db * 512:(db + 1) * 512, ic * 128:(ic + 1) * 128],
                            transpose=True), waits=(ww if ic == 0 else []), inc=LW2[bi], inck=16)
                    t_w2 = p.add("sync", lambda e, db=db, bi=bi: e.dma_start(
                        out=sb2[bi][:], in_=bass.AP(s2d_t, db * 512, [[0, 128], [1, 512]])),
                        inc=LW2[bi], inck=16)
                    for tt in range(8):
                        di = n_out_units % 2
                        tok_t = tbg * 8 + tt
                        mw = [(LW2[bi], t_w2), (LE, t_qhT)]
                        if n_out_units >= 2:
                            mw.append((sDVE, ev_d[n_out_units - 2]))
                        for ic in range(NIC):
                            t_md = p.add("tensor", lambda e, di=di, bi=bi, ic=ic, tt=tt: e.matmul(
                                pd5[di][:], qhT[:, ic * 1024 + tt * 128: ic * 1024 + (tt + 1) * 128],
                                w2s[bi][:, ic * 512:(ic + 1) * 512],
                                start=(ic == 0), stop=(ic == NIC - 1)),
                                waits=(mw if ic == 0 else []), inc=(sPE if ic == NIC - 1 else None))
                        pe_db[bi] = t_md
                        pe_sb_end = t_md
                        lw5 = [(sDVE, ld_x[n_out_units - 2])] if n_out_units >= 2 else []
                        t_lx = p.add("sync", lambda e, tok_t=tok_t, db=db, di=di: e.dma_start(
                            out=xa[di][:], in_=x_in[tok_t * 128:(tok_t + 1) * 128,
                                                    db * 512:(db + 1) * 512]),
                            waits=lw5, inc=LXA[di], inck=16)
                        t_la = p.add("sync", lambda e, tok_t=tok_t, db=db, di=di: e.dma_start(
                            out=ab[di][:], in_=abuf[tok_t * 128:(tok_t + 1) * 128,
                                                    db * 512:(db + 1) * 512]),
                            inc=LXA[di], inck=16)
                        t_fd = p.add("vector", lambda e, di=di, bi=bi, tok_t=tok_t: e.scalar_tensor_tensor(
                            out=fd[di][:], in0=pd5[di][:], scalar=sh_all[:, tok_t:tok_t + 1],
                            in1=sb2[bi][:], op0=ALU.mult, op1=ALU.mult),
                            waits=[(sPE, t_md)], inc=sDVE)
                        ev_d[n_out_units] = t_fd
                        db_dve[bi] = t_fd
                        t_t1 = p.add("vector", lambda e, di=di: e.tensor_add(
                            out=t1b[di][:], in0=xa[di][:], in1=ab[di][:]),
                            waits=[(LXA[di], t_la)], inc=sDVE)
                        ld_x[n_out_units] = t_t1
                        ow5 = [(sDVE, t_t1), (sDVE, t_fd)]
                        if n_out_units >= 2:
                            ow5.append((TOU[di], out_st[n_out_units - 2]))
                        t_o = p.add("vector", lambda e, di=di: e.tensor_add(
                            out=ob5[di][:], in0=t1b[di][:], in1=fd[di][:]),
                            waits=ow5, inc=sDVE)
                        t_os = p.add("gpsimd", lambda e, di=di, tok_t=tok_t, db=db: e.dma_start(
                            out=out[tok_t * 128:(tok_t + 1) * 128, db * 512:(db + 1) * 512],
                            in_=ob5[di][:]), waits=[(sDVE, t_o)], inc=TOU[di], inck=16)
                        out_st[n_out_units] = t_os
                        n_out_units += 1
            tot_e = TE.n
            tot_f = TF.n
            p.add("gpsimd", lambda e, tot_e=tot_e: e.wait_ge(TE.h, tot_e))
            p.add("gpsimd", lambda e, tot_f=tot_f: e.wait_ge(TF.h, tot_f))
            p.emit(nc)

        if DEBUG:
            p = Plan()
            for nm, (dst, src) in dbg.items():
                p.add("gpsimd", lambda e, dst=dst, src=src: e.dma_start(out=dst, in_=src),
                      inc=TE, inck=16)
            tot = TE.n
            p.add("gpsimd", lambda e, tot=tot: e.wait_ge(TE.h, tot))
            p.emit(nc)

    return nc


def _masks_np(half):
    m = np.zeros((6, 128, 512), dtype=np.float32)
    i = np.arange(128)[:, None]
    j = np.arange(512)[None, :]
    for d in range(4):
        m[d] = (i <= j - d * 128).astype(np.float32)
    m[4] = float(half)
    m[5] = float(half)
    return m.astype(ml_dtypes.bfloat16)


def _own_rows(half):
    return np.concatenate([np.arange(512 * (2 * k + half), 512 * (2 * k + half) + 512)
                           for k in range(4)])


def kernel(L0_attn_x, L0_attn_Wqkv, L0_attn_bqkv, L0_attn_Wo, L0_attn_bo,
           L0_ffn_w_gate_up, L0_ffn_w_down):
    x = np.ascontiguousarray(np.asarray(L0_attn_x, dtype=np.float32))
    wqkv = np.ascontiguousarray(np.asarray(L0_attn_Wqkv, dtype=np.float32))
    bqkv = np.ascontiguousarray(np.asarray(L0_attn_bqkv, dtype=np.float32))
    wo = np.ascontiguousarray(np.asarray(L0_attn_Wo, dtype=np.float32))
    bo = np.ascontiguousarray(np.asarray(L0_attn_bo, dtype=np.float32))
    w1 = np.ascontiguousarray(np.asarray(L0_ffn_w_gate_up, dtype=np.float32))
    w2 = np.ascontiguousarray(np.asarray(L0_ffn_w_down, dtype=np.float32))

    bqkv2 = np.ascontiguousarray(bqkv.reshape(48, 128).T)
    bo1 = np.ascontiguousarray(bo.reshape(1, D))
    masks = [_masks_np(h) for h in range(2)]
    rows = [_own_rows(h) for h in range(2)]

    nc = build_nc()
    in_maps = []
    for c in range(NCORE):
        b, h = c // 2, c % 2
        xp = np.ascontiguousarray(
            np.concatenate([x[b][rows[h]], x[b][rows[1 - h]]], axis=0))
        in_maps.append(dict(
            x=xp, wqkv=wqkv, bqkv2=bqkv2, wo=wo,
            bo=bo1, w1=w1, w2=w2, masks=masks[h]))
    res = run_bass_kernel_spmd(nc, in_maps, core_ids=list(range(NCORE)))
    out_full = np.empty((B, S, D), dtype=np.float32)
    for c in range(NCORE):
        b, h = c // 2, c % 2
        out_full[b][rows[h]] = np.asarray(res.results[c]["out"])
    return out_full


# version 16 (final)
# speedup vs baseline: 1.0371x; 1.0371x over previous
import contextlib
import numpy as np
import ml_dtypes

import concourse.bass as bass
import concourse.mybir as mybir
from concourse.bass_utils import run_bass_kernel_spmd

F32 = mybir.dt.float32
BF16 = mybir.dt.bfloat16
I32 = mybir.dt.int32
AF = mybir.ActivationFunctionType
ALU = mybir.AluOpType
AX = mybir.AxisListType

B, S, D = 4, 4096, 2048
NH, HD = 16, 128
DFF = 5632
NCORE = 8
QSC = 1.0 / float(np.sqrt(128.0))
INV127 = float(np.float32(1.0 / 127.0))
EPS_LN = 1e-5
EPS_Q = 1e-8
NT = S // 128
OS = S // 2
ONT = OS // 128
NIB = DFF // 512
NIC = DFF // 128

DEBUG = False


class Ctr:
    def __init__(self, h):
        self.h = h
        self.n = 0

    def bump(self, k):
        self.n += k
        return self.n


class Plan:
    ENGINES = ("sync", "tensor", "scalar", "vector", "gpsimd")

    def __init__(self):
        self.ops = {e: [] for e in self.ENGINES}

    def add(self, eng, emit, waits=(), inc=None, inck=1):
        v = inc.bump(inck) if inc is not None else None
        self.ops[eng].append((emit, [w for w in waits if w is not None], inc, inck))
        return v

    def emit(self, nc):
        with nc.Block() as block:
            for eng_name in self.ENGINES:
                ops = self.ops[eng_name]
                if not ops:
                    continue

                def make(ops_local):
                    def body(eng):
                        for emit, waits, inc, inck in ops_local:
                            for (c, v) in waits:
                                eng.wait_ge(c.h, v)
                            inst = emit(eng)
                            if inc is not None:
                                inst.then_inc(inc.h, inck)
                    return body

                getattr(block, eng_name)(make(ops))


def build_nc():
    nc = bass.Bass("TRN2", target_bir_lowering=False)

    x_in = nc.dram_tensor("x", [S, D], F32, kind="ExternalInput").ap()
    wqkv = nc.dram_tensor("wqkv", [D, 3 * D], F32, kind="ExternalInput").ap()
    bqkv2 = nc.dram_tensor("bqkv2", [128, 48], F32, kind="ExternalInput").ap()
    wo_in = nc.dram_tensor("wo", [D, D], F32, kind="ExternalInput").ap()
    bo_t = nc.dram_tensor("bo", [1, D], F32, kind="ExternalInput")
    bo_in = bo_t.ap()
    w1_in = nc.dram_tensor("w1", [2 * DFF, D], F32, kind="ExternalInput").ap()
    w2_in = nc.dram_tensor("w2", [D, DFF], F32, kind="ExternalInput").ap()
    masks_in = nc.dram_tensor("masks", [6, 128, 512], BF16, kind="ExternalInput").ap()
    out = nc.dram_tensor("out", [OS, D], F32, kind="ExternalOutput").ap()

    xn_tok = nc.dram_tensor("xn_tok", [S, D], BF16, kind="Internal").ap()
    qx_st = nc.dram_tensor("qx_st", [OS, D], BF16, kind="Internal").ap()
    kqvT = nc.dram_tensor("kqvT", [3 * D, S], BF16, kind="Internal").ap()
    otT = nc.dram_tensor("otT", [D, OS], BF16, kind="Internal").ap()
    abuf = nc.dram_tensor("abuf", [OS, D], F32, kind="Internal").ap()
    h_st = nc.dram_tensor("h_st", [OS, DFF], BF16, kind="Internal").ap()
    qh_st = nc.dram_tensor("qh_st", [OS, DFF], BF16, kind="Internal").ap()
    w1q_st = nc.dram_tensor("w1q_st", [2 * DFF, D], BF16, kind="Internal").ap()
    w2q_st = nc.dram_tensor("w2q_st", [D, DFF], BF16, kind="Internal").ap()
    s1d_t = nc.dram_tensor("s1d", [2 * DFF], F32, kind="Internal")
    s2d_t = nc.dram_tensor("s2d", [D], F32, kind="Internal")

    dbg = {}
    if DEBUG:
        for nm, src, shp, dt in [
            ("d_xn", xn_tok, [S, D], BF16), ("d_qx", qx_st, [OS, D], BF16),
            ("d_kqvT", kqvT, [3 * D, S], BF16), ("d_otT", otT, [D, OS], BF16),
            ("d_abuf", abuf, [OS, D], F32), ("d_qh", qh_st, [OS, DFF], BF16),
            ("d_w1q", w1q_st, [2 * DFF, D], BF16),
            ("d_w2q", w2q_st, [D, DFF], BF16),
            ("d_h", h_st, [OS, DFF], BF16),
        ]:
            dbg[nm] = (nc.dram_tensor(nm, shp, dt, kind="ExternalOutput").ap(), src)
        dbg["d_s1"] = (nc.dram_tensor("d_s1", [2 * DFF], F32, kind="ExternalOutput").ap(), s1d_t.ap())
        dbg["d_s2"] = (nc.dram_tensor("d_s2", [D], F32, kind="ExternalOutput").ap(), s2d_t.ap())

    with contextlib.ExitStack() as top:
        LA = Ctr(top.enter_context(nc.semaphore("LA")))
        LB = Ctr(top.enter_context(nc.semaphore("LB")))
        LC = Ctr(top.enter_context(nc.semaphore("LC")))
        LD = Ctr(top.enter_context(nc.semaphore("LD")))
        LE = Ctr(top.enter_context(nc.semaphore("LE")))
        LF = Ctr(top.enter_context(nc.semaphore("LF")))
        TA = Ctr(top.enter_context(nc.semaphore("TA")))
        TB = Ctr(top.enter_context(nc.semaphore("TB")))
        TC = Ctr(top.enter_context(nc.semaphore("TC")))
        TD = Ctr(top.enter_context(nc.semaphore("TD")))
        TE = Ctr(top.enter_context(nc.semaphore("TE")))
        TF = Ctr(top.enter_context(nc.semaphore("TF")))
        LI = Ctr(top.enter_context(nc.semaphore("LI")))
        LJ = Ctr(top.enter_context(nc.semaphore("LJ")))
        sPE = Ctr(top.enter_context(nc.semaphore("sPE")))
        sACT = Ctr(top.enter_context(nc.semaphore("sACT")))
        sDVE = Ctr(top.enter_context(nc.semaphore("sDVE")))
        sPOOL = Ctr(top.enter_context(nc.semaphore("sPOOL")))
        STORES = (TA, TB, TC, TD, TE, TF)

        def entry():
            return [(c, c.n) for c in STORES if c.n > 0]

        bq_sb = top.enter_context(nc.sbuf_tensor("bq_sb", [128, 48], F32))
        bqs_sb = top.enter_context(nc.sbuf_tensor("bqs_sb", [128, 48], F32))
        masks_sb = top.enter_context(nc.sbuf_tensor("masks_sb", [128, 6 * 512], BF16))
        ones_c = top.enter_context(nc.sbuf_tensor("ones_c", [128, 1], F32))
        ones_r = top.enter_context(nc.sbuf_tensor("ones_r", [1, 128], F32))
        ones_bb = top.enter_context(nc.sbuf_tensor("ones_bb", [128, 128], BF16))
        sx_all = top.enter_context(nc.sbuf_tensor("sx_all", [128, NT], F32))
        sh_all = top.enter_context(nc.sbuf_tensor("sh_all", [128, NT], F32))
        eps_c = top.enter_context(nc.sbuf_tensor("eps_c", [128, 1], F32))
        prb6 = top.enter_context(nc.sbuf_tensor("prb6", [128, 2], F32))
        prbi = top.enter_context(nc.sbuf_tensor("prbi", [128, 2], I32))
        prbf = top.enter_context(nc.sbuf_tensor("prbf", [128, 2], F32))
        qr_sc = top.enter_context(nc.sbuf_tensor("qr_sc", [128, 4], F32))
        ab_sb = top.enter_context(nc.sbuf_tensor("ab_sb", [128, 4], F32))

        p = Plan()
        p.add("sync", lambda e: e.dma_start(out=bq_sb[:], in_=bqkv2), inc=LA, inck=16)
        for d_ in range(6):
            p.add("sync", lambda e, d_=d_: e.dma_start(
                out=masks_sb[:, d_ * 512:(d_ + 1) * 512], in_=masks_in[d_]),
                inc=LA, inck=16)
        t_ld = LA.n
        p.add("vector", lambda e: e.memset(ones_c[:], 1.0))
        p.add("vector", lambda e: e.memset(ones_r[:], 1.0))
        p.add("vector", lambda e: e.memset(ones_bb[:], 1.0))
        p.add("vector", lambda e: e.memset(eps_c[:], EPS_LN))
        p.add("vector", lambda e: e.tensor_scalar(
            out=bqs_sb[:, 0:16], in0=bq_sb[:, 0:16], scalar1=QSC, scalar2=None,
            op0=ALU.mult), waits=[(LA, t_ld)])
        p.add("vector", lambda e: e.tensor_copy(out=bqs_sb[:, 16:48], in_=bq_sb[:, 16:48]),
              waits=[(LA, t_ld)])
        g1 = p.add("vector", lambda e: e.memset(prb6[:, 0:1], 0.6), inc=sDVE)
        g2 = p.add("vector", lambda e: e.memset(prb6[:, 1:2], -0.6),
                   waits=[(sDVE, g1)], inc=sDVE)
        g3 = p.add("vector", lambda e: e.tensor_scalar(
            out=prbi[:], in0=prb6[:], scalar1=0.0, scalar2=None, op0=ALU.add),
            waits=[(sDVE, g2)], inc=sDVE)
        g4 = p.add("vector", lambda e: e.tensor_copy(out=prbf[:], in_=prbi[:]),
                   waits=[(sDVE, g3)], inc=sDVE)
        g5 = p.add("vector", lambda e: e.tensor_scalar(
            out=qr_sc[:, 0:1], in0=prbf[:, 0:1], scalar1=-1.0, scalar2=1.0,
            op0=ALU.mult, op1=ALU.add), waits=[(sDVE, g4)], inc=sDVE)
        g6 = p.add("vector", lambda e: e.tensor_scalar(
            out=qr_sc[:, 1:2], in0=prbf[:, 1:2], scalar1=-1.0, scalar2=None,
            op0=ALU.mult), waits=[(sDVE, g5)], inc=sDVE)
        g7 = p.add("vector", lambda e: e.tensor_scalar(
            out=qr_sc[:, 2:3], in0=qr_sc[:, 1:2], scalar1=-1.0, scalar2=1.0,
            op0=ALU.mult, op1=ALU.add), waits=[(sDVE, g6)], inc=sDVE)
        g8 = p.add("vector", lambda e: e.tensor_mul(
            out=qr_sc[:, 3:4], in0=qr_sc[:, 0:1], in1=qr_sc[:, 1:2]),
            waits=[(sDVE, g7)], inc=sDVE)
        g9 = p.add("vector", lambda e: e.tensor_mul(
            out=ab_sb[:, 0:1], in0=qr_sc[:, 0:1], in1=qr_sc[:, 2:3]),
            waits=[(sDVE, g8)], inc=sDVE)
        g10 = p.add("vector", lambda e: e.tensor_scalar(
            out=ab_sb[:, 0:1], in0=ab_sb[:, 0:1], scalar1=0.5, scalar2=None,
            op0=ALU.mult), waits=[(sDVE, g9)], inc=sDVE)
        g11 = p.add("vector", lambda e: e.tensor_scalar(
            out=ab_sb[:, 1:2], in0=qr_sc[:, 3:4], scalar1=0.5, scalar2=None,
            op0=ALU.mult), waits=[(sDVE, g10)], inc=sDVE)
        g12 = p.add("vector", lambda e: e.tensor_scalar(
            out=ab_sb[:, 2:3], in0=ab_sb[:, 0:1], scalar1=2.0, scalar2=None,
            op0=ALU.mult), waits=[(sDVE, g11)], inc=sDVE)
        g13 = p.add("vector", lambda e: e.tensor_sub(
            out=ab_sb[:, 3:4], in0=ab_sb[:, 1:2], in1=ab_sb[:, 0:1]),
            waits=[(sDVE, g12)], inc=sDVE)
        p.emit(nc)
        al_ap = ab_sb[:, 0:1]
        be_ap = ab_sb[:, 1:2]
        ga_ap = ab_sb[:, 2:3]
        dm_ap = ab_sb[:, 3:4]

        with contextlib.ExitStack() as es:
            xt = [es.enter_context(nc.sbuf_tensor(f"s1_x{i}", [128, D], F32)) for i in range(2)]
            x2 = [es.enter_context(nc.sbuf_tensor(f"s1_x2{i}", [128, D], F32)) for i in range(2)]
            s1s = es.enter_context(nc.sbuf_tensor("s1_s1", [128, NT], F32))
            s2s = es.enter_context(nc.sbuf_tensor("s1_s2", [128, NT], F32))
            negm = es.enter_context(nc.sbuf_tensor("s1_negm", [128, NT], F32))
            m2 = es.enter_context(nc.sbuf_tensor("s1_m2", [128, NT], F32))
            varb = es.enter_context(nc.sbuf_tensor("s1_var", [128, NT], F32))
            sdv = es.enter_context(nc.sbuf_tensor("s1_sd", [128, NT], F32))
            rstd = es.enter_context(nc.sbuf_tensor("s1_rstd", [128, NT], F32))
            nmr = es.enter_context(nc.sbuf_tensor("s1_nmr", [128, NT], F32))
            xnf = [es.enter_context(nc.sbuf_tensor(f"s1_xnf{i}", [128, D], F32)) for i in range(2)]
            xnb = [es.enter_context(nc.sbuf_tensor(f"s1_xnb{i}", [128, D], BF16)) for i in range(2)]
            amq = es.enter_context(nc.sbuf_tensor("s1_amq", [128, 2], F32))
            rq1 = es.enter_context(nc.sbuf_tensor("s1_rq", [128, 2], F32))
            yq1 = [es.enter_context(nc.sbuf_tensor(f"s1_yq{i}", [128, D], F32)) for i in range(2)]
            ti1 = [es.enter_context(nc.sbuf_tensor(f"s1_ti{i}", [128, D], I32)) for i in range(2)]
            qn1 = [es.enter_context(nc.sbuf_tensor(f"s1_qn{i}", [128, D], BF16)) for i in range(2)]
            sg1 = [es.enter_context(nc.sbuf_tensor(f"s1_sg{i}", [128, D], BF16)) for i in range(2)]
            p = Plan()
            LX = [LA, LB]
            TXN = [TA, TB]
            TQX = [TC, TD]
            tA_dve = [0, 0]
            for t in range(NT):
                bi = t % 2
                lw0 = entry() if t == 0 else []
                t_l = p.add("sync", lambda e, t=t, bi=bi: e.dma_start(
                    out=xt[bi][:], in_=x_in[t * 128:(t + 1) * 128, :]),
                    waits=lw0 + ([(sDVE, tA_dve[bi])] if t >= 2 else []),
                    inc=LX[bi], inck=16)
                t_sq = p.add("scalar", lambda e, bi=bi: e.activation(
                    out=x2[bi][:], in_=xt[bi][:], func=AF.Square),
                    waits=[(LX[bi], t_l)], inc=sACT)
                p.add("vector", lambda e, t=t, bi=bi: e.reduce_sum(
                    out=s1s[:, t:t + 1], in_=xt[bi][:], axis=AX.X),
                    waits=[(LX[bi], t_l)], inc=sDVE)
                t_r2 = p.add("vector", lambda e, t=t, bi=bi: e.reduce_sum(
                    out=s2s[:, t:t + 1], in_=x2[bi][:], axis=AX.X),
                    waits=[(sACT, t_sq)], inc=sDVE)
                tA_dve[bi] = t_r2
            t1 = p.add("vector", lambda e: e.tensor_scalar(
                out=negm[:], in0=s1s[:], scalar1=-1.0 / D, scalar2=None, op0=ALU.mult),
                waits=[(sDVE, sDVE.n)], inc=sDVE)
            t2 = p.add("scalar", lambda e: e.activation(
                out=m2[:], in_=s1s[:], func=AF.Square, scale=1.0 / D),
                waits=[(sDVE, t1)], inc=sACT)
            t3 = p.add("vector", lambda e: e.scalar_tensor_tensor(
                out=varb[:], in0=s2s[:], scalar=1.0 / D, in1=m2[:],
                op0=ALU.mult, op1=ALU.subtract), waits=[(sACT, t2)], inc=sDVE)
            t4 = p.add("scalar", lambda e: e.activation(
                out=sdv[:], in_=varb[:], func=AF.Sqrt, bias=eps_c[:]),
                waits=[(sDVE, t3)], inc=sACT)
            t5 = p.add("vector", lambda e: e.reciprocal(out=rstd[:], in_=sdv[:]),
                       waits=[(sACT, t4)], inc=sDVE)
            t6 = p.add("vector", lambda e: e.tensor_mul(out=nmr[:], in0=negm[:], in1=rstd[:]),
                       waits=[(sDVE, t5)], inc=sDVE)
            tB_act = [0, 0]
            tB_st = [0, 0]
            tB_xnst = [0, 0]
            for t in range(NT):
                bi = t % 2
                quant = t < ONT
                t_l = p.add("sync", lambda e, t=t, bi=bi: e.dma_start(
                    out=xt[bi][:], in_=x_in[t * 128:(t + 1) * 128, :]),
                    waits=([(sACT, tB_act[bi])] if t >= 2
                           else [(sDVE, tA_dve[bi])]), inc=LX[bi], inck=16)
                t_xn = p.add("scalar", lambda e, t=t, bi=bi: e.activation(
                    out=xnf[bi][:], in_=xt[bi][:], func=AF.Identity,
                    scale=rstd[:, t:t + 1], bias=nmr[:, t:t + 1]),
                    waits=[(LX[bi], t_l), (sDVE, t6)]
                          + ([(TQX[bi], tB_st[bi])] if 2 <= t < ONT + 2 else [])
                          + ([(TXN[bi], tB_xnst[bi])] if t >= ONT + 2 else []),
                    inc=sACT)
                t_bf = p.add("vector", lambda e, bi=bi: e.tensor_copy(
                    out=xnb[bi][:], in_=xnf[bi][:]),
                    waits=[(sACT, t_xn)] + ([(TXN[bi], tB_xnst[bi])] if t >= 2 else []),
                    inc=sDVE)
                t_xst = p.add("gpsimd", lambda e, t=t, bi=bi: e.dma_start(
                    out=xn_tok[t * 128:(t + 1) * 128, :], in_=xnb[bi][:]),
                    waits=[(sDVE, t_bf)], inc=TXN[bi], inck=16)
                tB_xnst[bi] = t_xst
                if not quant:
                    tB_act[bi] = t_xn
                    continue
                t_am = p.add("vector", lambda e, bi=bi: e.reduce_max(
                    out=amq[:, bi:bi + 1], in_=xnf[bi][:], axis=AX.X,
                    apply_absolute_value=True), waits=[(sACT, t_xn)], inc=sDVE)
                t_s = p.add("vector", lambda e, t=t, bi=bi: e.tensor_scalar(
                    out=sx_all[:, t:t + 1], in0=amq[:, bi:bi + 1], scalar1=INV127,
                    scalar2=EPS_Q, op0=ALU.mult, op1=ALU.max),
                    waits=[(sDVE, t_am)], inc=sDVE)
                t_r = p.add("vector", lambda e, t=t, bi=bi: e.reciprocal(
                    out=rq1[:, bi:bi + 1], in_=sx_all[:, t:t + 1]),
                    waits=[(sDVE, t_s)] + ([(sACT, tB_act[bi])] if t >= 2 else []),
                    inc=sDVE)
                t_sg = p.add("scalar", lambda e, bi=bi: e.activation(
                    out=sg1[bi][:], in_=xnf[bi][:], func=AF.Sign),
                    waits=[(sDVE, t_r)], inc=sACT)
                t_y = p.add("scalar", lambda e, bi=bi: e.activation(
                    out=yq1[bi][:], in_=xnf[bi][:], func=AF.Identity,
                    scale=rq1[:, bi:bi + 1], bias=be_ap),
                    waits=[(sDVE, t_r)], inc=sACT)
                tB_act[bi] = t_y
                t_i = p.add("vector", lambda e, bi=bi: e.scalar_tensor_tensor(
                    out=ti1[bi][:], in0=sg1[bi][:], scalar=al_ap,
                    in1=yq1[bi][:], op0=ALU.mult, op1=ALU.add),
                    waits=[(sACT, t_y)], inc=sDVE)
                t_q = p.add("vector", lambda e, bi=bi: e.tensor_scalar(
                    out=qn1[bi][:], in0=ti1[bi][:], scalar1=-1.0, scalar2=None,
                    op0=ALU.mult),
                    waits=[(sDVE, t_i)] + ([(TQX[bi], tB_st[bi])] if t >= 2 else []),
                    inc=sDVE)
                t_st = p.add("gpsimd", lambda e, t=t, bi=bi: e.dma_start(
                    out=qx_st[t * 128:(t + 1) * 128, :], in_=qn1[bi][:]),
                    waits=[(sDVE, t_q)], inc=TQX[bi], inck=16)
                tB_st[bi] = t_st
            p.emit(nc)

        with contextlib.ExitStack() as es:
            xnT = es.enter_context(nc.sbuf_tensor("s2_xnT", [128, 16 * S], BF16, is_transpose_dest=True))
            wqf = [es.enter_context(nc.sbuf_tensor(f"s2_wqf{i}", [128, 16 * 128], F32)) for i in range(2)]
            wqb = [es.enter_context(nc.sbuf_tensor(f"s2_wqb{i}", [128, 16 * 128], BF16)) for i in range(2)]
            kq = [es.enter_context(nc.sbuf_tensor(f"s2_kq{i}", [128, 512], BF16)) for i in range(4)]
            ps2 = [es.enter_context(nc.psum_tensor(f"s2_ps{i}", [128, 512], F32)) for i in range(4)]
            p = Plan()
            LWQ = [LD, LE]
            TKQ = [TA, TB, TC, TD]
            for dc in range(16):
                for tb in range(8):
                    p.add("sync", lambda e, dc=dc, tb=tb: e.dma_start(
                        out=xnT[:, dc * S + tb * 512: dc * S + (tb + 1) * 512],
                        in_=xn_tok[tb * 512:(tb + 1) * 512, dc * 128:(dc + 1) * 128],
                        transpose=True),
                        waits=(entry() if (dc == 0 and tb == 0) else []),
                        inc=LC, inck=16)
            t_xnT = LC.n
            tw_cast = [0, 0]
            pe_oc = [0, 0]
            tk_evac = {}
            tk_st = {}
            unit = 0
            for oc in range(48):
                bi = oc % 2
                for dc in range(16):
                    t_wl = p.add("sync", lambda e, oc=oc, dc=dc, bi=bi: e.dma_start(
                        out=wqf[bi][:, dc * 128:(dc + 1) * 128],
                        in_=wqkv[dc * 128:(dc + 1) * 128, oc * 128:(oc + 1) * 128]),
                        waits=([(sDVE, tw_cast[bi])] if (oc >= 2 and dc == 0) else []),
                        inc=LWQ[bi], inck=16)
                t_c = p.add("vector", lambda e, bi=bi: e.tensor_copy(
                    out=wqb[bi][:], in_=wqf[bi][:]),
                    waits=[(LWQ[bi], t_wl)] + ([(sPE, pe_oc[bi])] if oc >= 2 else []),
                    inc=sDVE)
                tw_cast[bi] = t_c
                qsc = QSC if oc < 16 else 1.0
                for tb in range(4 if oc < 16 else 8):
                    bank = unit % 4
                    mm_w = [(sDVE, t_c), (LC, t_xnT)]
                    if unit >= 4:
                        mm_w.append((sACT, tk_evac[unit - 4]))
                    for dc in range(16):
                        t_mm = p.add("tensor", lambda e, bank=bank, bi=bi, dc=dc, tb=tb: e.matmul(
                            ps2[bank][:], wqb[bi][:, dc * 128:(dc + 1) * 128],
                            xnT[:, dc * S + tb * 512: dc * S + (tb + 1) * 512],
                            start=(dc == 0), stop=(dc == 15)),
                            waits=(mm_w if dc == 0 else []),
                            inc=(sPE if dc == 15 else None))
                    t_ev = p.add("scalar", lambda e, bank=bank, oc=oc, qsc=qsc: e.activation(
                        out=kq[bank][:], in_=ps2[bank][:], func=AF.Identity,
                        scale=qsc, bias=bqs_sb[:, oc:oc + 1]),
                        waits=[(sPE, t_mm)] + ([(TKQ[bank], tk_st[unit - 4])] if unit >= 4 else []),
                        inc=sACT)
                    tk_evac[unit] = t_ev
                    t_st = p.add("gpsimd", lambda e, bank=bank, oc=oc, tb=tb: e.dma_start(
                        out=kqvT[oc * 128:(oc + 1) * 128, tb * 512:(tb + 1) * 512],
                        in_=kq[bank][:]), waits=[(sACT, t_ev)], inc=TKQ[bank], inck=16)
                    tk_st[unit] = t_st
                    pe_oc[bi] = t_mm
                    unit += 1
            p.emit(nc)

        with contextlib.ExitStack() as es:
            QT = [es.enter_context(nc.sbuf_tensor(f"s3_QT{i}", [128, OS], BF16)) for i in range(2)]
            KT = [es.enter_context(nc.sbuf_tensor(f"s3_KT{i}", [128, S], BF16)) for i in range(2)]
            VT = [es.enter_context(nc.sbuf_tensor(f"s3_VT{i}", [128, S], BF16, is_transpose_dest=True)) for i in range(2)]
            ptall = es.enter_context(nc.sbuf_tensor("s3_pt", [128, 32 * 512], BF16))
            rbs = [es.enter_context(nc.sbuf_tensor(f"s3_rbs{i}", [128, 512], F32)) for i in range(2)]
            otb = [es.enter_context(nc.sbuf_tensor(f"s3_otb{i}", [128, 512], BF16)) for i in range(2)]
            ssp = [es.enter_context(nc.psum_tensor(f"s3_ssp{i}", [128, 1024], F32)) for i in range(2)]
            pot = [es.enter_context(nc.psum_tensor(f"s3_ot{i}", [128, 512], F32)) for i in range(2)]
            pdn = [es.enter_context(nc.psum_tensor(f"s3_dn{i}", [128, 512], F32)) for i in range(2)]
            CW = 1408
            wt = [es.enter_context(nc.sbuf_tensor(f"s0_wt{i}", [128, DFF], F32)) for i in range(2)]
            qn0 = [es.enter_context(nc.sbuf_tensor(f"s0_qn{i}", [128, DFF], BF16)) for i in range(2)]
            yq0 = [es.enter_context(nc.sbuf_tensor(f"s0_yq{i}", [128, CW], F32)) for i in range(4)]
            sg0 = [es.enter_context(nc.sbuf_tensor(f"s0_sg{i}", [128, CW], BF16)) for i in range(4)]
            ti0 = [es.enter_context(nc.sbuf_tensor(f"s0_ti{i}", [128, CW], I32)) for i in range(2)]
            am_all = es.enter_context(nc.sbuf_tensor("s0_am", [128, 104], F32))
            sc_all = es.enter_context(nc.sbuf_tensor("s0_sc", [128, 104], F32))
            rc_all = es.enter_context(nc.sbuf_tensor("s0_rc", [128, 104], F32))
            p = Plan()
            LH = [LA, LB]
            TOT = [TA, TB]

            NR1 = 2 * DFF // 128
            NR2 = D // 128
            NR = NR1 + NR2
            rowinfo = []
            for r in range(NR1):
                rowinfo.append((w1_in, w1q_st, s1d_t, r, D, 512))
            for r in range(NR2):
                rowinfo.append((w2_in, w2q_st, s2d_t, r, DFF, 1408))
            LW = [LI, LJ]
            TQ = [TC, TD]
            s0_load = {}
            s0_rc = {}
            s0_am = {}
            s0_ylast = {}
            s0_y = {}
            s0_stt = {}
            s0_neg_last = {}
            s0_store = {}

            s0_fifo = []
            s0_glast = {}
            s0_g = {}

            def s0_A(r):
                w_ap, _, _, rr, ncol, cw = rowinfo[r]
                bi = r % 2

                def go(w_ap=w_ap, rr=rr, bi=bi, ncol=ncol, r=r):
                    lw = []
                    if r >= 2:
                        lw.append((sACT, s0_ylast[r - 2]))
                        lw.append((sDVE, s0_glast[r - 2]))
                    s0_load[r] = p.add("gpsimd", lambda e: e.dma_start(
                        out=wt[bi][:, 0:ncol], in_=w_ap[rr * 128:(rr + 1) * 128, :]),
                        waits=lw, inc=LW[bi], inck=16)
                s0_fifo.append(go)

            def s0_B(r):
                _, _, sd_t, rr, ncol, cw = rowinfo[r]
                bi = r % 2

                def go(sd_t=sd_t, rr=rr, ncol=ncol, bi=bi, r=r):
                    t_am = p.add("vector", lambda e: e.reduce_max(
                        out=am_all[:, r:r + 1], in_=wt[bi][:, 0:ncol], axis=AX.X,
                        apply_absolute_value=True), waits=[(LW[bi], s0_load[r])], inc=sDVE)
                    s0_am[r] = t_am
                    t_s = p.add("vector", lambda e: e.tensor_scalar(
                        out=sc_all[:, r:r + 1], in0=am_all[:, r:r + 1], scalar1=INV127,
                        scalar2=EPS_Q, op0=ALU.mult, op1=ALU.max),
                        waits=[(sDVE, t_am)], inc=sDVE)
                    t_rc = p.add("vector", lambda e: e.reciprocal(
                        out=rc_all[:, r:r + 1], in_=sc_all[:, r:r + 1]),
                        waits=[(sDVE, t_s)], inc=sDVE)
                    s0_rc[r] = t_rc
                    p.add("gpsimd", lambda e: e.dma_start(
                        out=bass.AP(sd_t, rr * 128, [[1, 128]]),
                        in_=sc_all[:, r:r + 1]),
                        waits=[(sDVE, t_s)], inc=TE, inck=16)
                s0_fifo.append(go)

            def s0_C(r):
                _, _, _, rr, ncol, cw = rowinfo[r]
                bi = r % 2
                nch = ncol // cw
                s0_y[r] = {}
                for c in range(nch):
                    sl = slice(c * cw, (c + 1) * cw)

                    def go(bi=bi, sl=sl, c=c, r=r, nch=nch):
                        sgw = [(LW[bi], s0_load[r])]
                        if r >= 1:
                            sgw.append((sDVE, s0_stt[r - 1][c]))
                        if c == 0:
                            sgw.append((sDVE, s0_rc[r]))
                        t_sg = p.add("vector", lambda e: e.tensor_scalar(
                            out=sg0[c][:, 0:sl.stop - sl.start], in0=wt[bi][:, sl],
                            scalar1=0.0, scalar2=None, op0=ALU.is_gt),
                            waits=sgw, inc=sDVE)
                        s0_glast[r] = t_sg
                        s0_g.setdefault(r, {})[c] = t_sg
                        t_y = p.add("scalar", lambda e: e.activation(
                            out=yq0[c][:, 0:sl.stop - sl.start], in_=wt[bi][:, sl],
                            func=AF.Identity, scale=rc_all[:, r:r + 1], bias=dm_ap),
                            waits=([(sDVE, s0_rc[r])] if c == 0 else []) +
                                  ([(sDVE, s0_stt[r - 1][c])] if r >= 1 else []),
                            inc=sACT)
                        s0_y[r][c] = t_y
                        if c == nch - 1:
                            s0_ylast[r] = t_y
                    s0_fifo.append(go)

            s0_ti_user = {}

            def s0_D(r):
                _, wq_ap, _, rr, ncol, cw = rowinfo[r]
                bi = r % 2
                nch = ncol // cw
                s0_stt[r] = {}
                for c in range(nch):
                    sl = slice(c * cw, (c + 1) * cw)

                    def go(bi=bi, sl=sl, c=c, r=r, nch=nch, wq_ap=wq_ap,
                           rr=rr, ncol=ncol):
                        iw = [(sACT, s0_y[r][c]), (sDVE, s0_g[r][c])]
                        if c % 2 in s0_ti_user:
                            iw.append((sACT, s0_ti_user[c % 2]))
                        t_i = p.add("vector", lambda e: e.scalar_tensor_tensor(
                            out=ti0[c % 2][:, 0:sl.stop - sl.start],
                            in0=sg0[c][:, 0:sl.stop - sl.start], scalar=ga_ap,
                            in1=yq0[c][:, 0:sl.stop - sl.start], op0=ALU.mult, op1=ALU.add),
                            waits=iw, inc=sDVE)
                        s0_stt[r][c] = t_i
                        nw = [(sDVE, t_i)]
                        if c == 0 and r >= 2:
                            nw.append((TQ[bi], s0_store[r - 2]))
                        t_neg = p.add("scalar", lambda e: e.activation(
                            out=qn0[bi][:, sl], in_=ti0[c % 2][:, 0:sl.stop - sl.start],
                            func=AF.Identity, scale=-1.0),
                            waits=nw, inc=sACT)
                        s0_ti_user[c % 2] = t_neg
                        if c == nch - 1:
                            s0_neg_last[r] = t_neg
                            s0_store[r] = p.add("gpsimd", lambda e: e.dma_start(
                                out=wq_ap[rr * 128:(rr + 1) * 128, :],
                                in_=qn0[bi][:, 0:ncol]),
                                waits=[(sACT, t_neg)], inc=TQ[bi], inck=16)
                    s0_fifo.append(go)

            s0_cursors = [0, 0, 0, 0]
            while s0_cursors != [NR, NR, NR, NR]:
                cur = s0_cursors
                before = list(cur)
                t = min(NR, cur[2])
                while cur[3] < t:
                    s0_D(cur[3]); cur[3] += 1
                t = min(NR, cur[0])
                while cur[1] < t:
                    s0_B(cur[1]); cur[1] += 1
                t = min(NR, cur[0], cur[1], cur[3] + 1)
                while cur[2] < t:
                    s0_C(cur[2]); cur[2] += 1
                t = min(NR, cur[2] + 2, cur[1] + 2)
                while cur[0] < t:
                    s0_A(cur[0]); cur[0] += 1
                if cur == before:
                    raise RuntimeError(f"s0 enqueue stuck at {cur}")
            s0_pos = [0]

            def s0_pump(k):
                n_ = min(len(s0_fifo) - s0_pos[0], k)
                for _ in range(n_):
                    s0_fifo[s0_pos[0]]()
                    s0_pos[0] += 1

            def s0_flush():
                s0_pump(len(s0_fifo))
            h_done = [0, 0]
            ld_tok = [0, 0]
            qk_tok = [0, 0]
            LVT = [LC, LF]
            prev_exp_last = 0
            d_prev = 0
            ot_evac_pb = [0, 0]
            rc_pb = [0, 0]
            st_tok3 = [0, 0]
            first_qb = True
            for n in range(NH):
                bi = n % 2
                lw = ([(sPE, h_done[bi])] if n >= 2
                      else (entry() if n == 0 else []))
                p.add("sync", lambda e, n=n, bi=bi: e.dma_start(
                    out=QT[bi][:], in_=kqvT[n * 128:(n + 1) * 128, 0:OS]),
                    waits=lw, inc=LH[bi], inck=16)
                p.add("sync", lambda e, n=n, bi=bi: e.dma_start(
                    out=KT[bi][:], in_=kqvT[D + n * 128: D + (n + 1) * 128, :]),
                    inc=LH[bi], inck=16)
                qk_tok[bi] = LH[bi].n
                for kc in range(32):
                    p.add("sync", lambda e, n=n, bi=bi, kc=kc: e.dma_start(
                        out=VT[bi][:, kc * 128:(kc + 1) * 128],
                        in_=kqvT[2 * D + n * 128: 2 * D + (n + 1) * 128,
                                 kc * 128:(kc + 1) * 128], transpose=True),
                        inc=LVT[bi], inck=16)
                ld_tok[bi] = LVT[bi].n

                for qb in range(4):
                    nown = 4 * (qb + 1)
                    nk = 2 * nown
                    npair = nk // 2
                    pb = (n * 4 + qb) % 2

                    def kaddr(kc, nown=nown):
                        if kc < nown:
                            return kc * 128
                        return OS + (kc - nown) * 128

                    exp_tok = {}
                    mask_tok = {}

                    def emit_avd(kc):
                        jp = kc // 2
                        avw = [(sACT, exp_tok[jp])]
                        if kc == 0:
                            avw.append((LVT[bi], ld_tok[bi]))
                        if jp in mask_tok:
                            avw.append((sDVE, mask_tok[jp]))
                        if kc == 0 and ot_evac_pb[pb]:
                            avw.append((sDVE, ot_evac_pb[pb]))
                        t = p.add("tensor", lambda e, kc=kc, nk=nk, bi=bi, pb=pb, ka=kaddr(kc): e.matmul(
                            pot[pb][:], VT[bi][:, ka:ka + 128],
                            ptall[:, kc * 512:(kc + 1) * 512],
                            start=(kc == 0), stop=(kc == nk - 1)),
                            waits=avw, inc=sPE)
                        dw = []
                        if kc == 0 and rc_pb[pb]:
                            dw.append((sDVE, rc_pb[pb]))
                        t = p.add("tensor", lambda e, kc=kc, nk=nk, pb=pb: e.matmul(
                            pdn[pb][:], ones_bb[:],
                            ptall[:, kc * 512:(kc + 1) * 512],
                            start=(kc == 0), stop=(kc == nk - 1)),
                            waits=dw, inc=sPE)
                        return t

                    t_d = 0
                    for jp in range(npair):
                        kc0 = 2 * jp
                        sb_ = jp % 2
                        ssw = [(LH[bi], qk_tok[bi])]
                        if jp >= 2:
                            ssw.append((sACT, exp_tok[jp - 2]))
                        elif not first_qb:
                            ssw.append((sACT, prev_exp_last))
                        if jp == 0 and not first_qb:
                            ssw.append((sPE, d_prev))
                        p.add("tensor", lambda e, sb_=sb_, qb=qb, bi=bi, ka=kaddr(kc0): e.matmul(
                            ssp[sb_][:, 0:512], KT[bi][:, ka:ka + 128],
                            QT[bi][:, qb * 512:(qb + 1) * 512], start=True, stop=True),
                            waits=ssw, inc=sPE)
                        t_ss = p.add("tensor", lambda e, sb_=sb_, qb=qb, bi=bi, ka=kaddr(kc0 + 1): e.matmul(
                            ssp[sb_][:, 512:1024], KT[bi][:, ka:ka + 128],
                            QT[bi][:, qb * 512:(qb + 1) * 512], start=True, stop=True),
                            inc=sPE)
                        t_e = p.add("scalar", lambda e, sb_=sb_, kc0=kc0: e.activation(
                            out=ptall[:, kc0 * 512:(kc0 + 2) * 512], in_=ssp[sb_][:],
                            func=AF.Exp), waits=[(sPE, t_ss)], inc=sACT)
                        exp_tok[jp] = t_e
                        d_ = None
                        if 2 * qb <= jp < 2 * qb + 2:
                            d_ = 2 * (jp - 2 * qb)
                        elif jp >= npair - 2:
                            d_ = 4
                        if d_ is not None:
                            t_m = p.add("vector", lambda e, kc0=kc0, d_=d_: e.tensor_mul(
                                out=ptall[:, kc0 * 512:(kc0 + 2) * 512],
                                in0=ptall[:, kc0 * 512:(kc0 + 2) * 512],
                                in1=masks_sb[:, d_ * 512:(d_ + 2) * 512]),
                                waits=[(sACT, t_e)], inc=sDVE)
                            mask_tok[jp] = t_m
                        if jp >= 1:
                            emit_avd(kc0 - 2)
                            emit_avd(kc0 - 1)
                        s0_pump(4)
                    emit_avd(nk - 2)
                    t_d = emit_avd(nk - 1)
                    d_prev = t_d
                    prev_exp_last = exp_tok[npair - 1]
                    first_qb = False
                    t_rc = p.add("vector", lambda e, pb=pb: e.reciprocal(
                        out=rbs[pb][:], in_=pdn[pb][:]),
                        waits=[(sPE, t_d)], inc=sDVE)
                    rc_pb[pb] = t_rc
                    ob = pb
                    ow = [(sDVE, t_rc)]
                    if n * 8 + qb >= 2:
                        ow.append((TOT[ob], st_tok3[ob]))
                    t_ob = p.add("vector", lambda e, ob=ob, pb=pb: e.tensor_mul(
                        out=otb[ob][:], in0=pot[pb][:], in1=rbs[pb][:]), waits=ow, inc=sDVE)
                    ot_evac_pb[pb] = t_ob
                    t_so = p.add("gpsimd", lambda e, n=n, qb=qb, ob=ob: e.dma_start(
                        out=otT[n * 128:(n + 1) * 128, qb * 512:(qb + 1) * 512],
                        in_=otb[ob][:]), waits=[(sDVE, t_ob)], inc=TOT[ob], inck=16)
                    st_tok3[ob] = t_so
                h_done[bi] = sPE.n
            s0_flush()
            p.emit(nc)

        with contextlib.ExitStack() as es:
            qxT = es.enter_context(nc.sbuf_tensor("s5_qxT", [128, 16 * OS], BF16, is_transpose_dest=True))
            w1g = [es.enter_context(nc.sbuf_tensor(f"s5_w1g{i}", [128, 16 * 512], BF16, is_transpose_dest=True)) for i in range(2)]
            w1u = [es.enter_context(nc.sbuf_tensor(f"s5_w1u{i}", [128, 16 * 512], BF16, is_transpose_dest=True)) for i in range(2)]
            sb1g = [es.enter_context(nc.sbuf_tensor(f"s5_sb1g{i}", [128, 512], F32)) for i in range(2)]
            sb1u = [es.enter_context(nc.sbuf_tensor(f"s5_sb1u{i}", [128, 512], F32)) for i in range(2)]
            gd = [es.enter_context(nc.sbuf_tensor(f"s5_gd{i}", [128, 512], F32)) for i in range(2)]
            sgt = [es.enter_context(nc.sbuf_tensor(f"s5_sg{i}", [128, 512], F32)) for i in range(2)]
            ud = [es.enter_context(nc.sbuf_tensor(f"s5_ud{i}", [128, 512], F32)) for i in range(2)]
            hb = [es.enter_context(nc.sbuf_tensor(f"s5_hb{i}", [128, 512], BF16)) for i in range(2)]
            pg = [es.enter_context(nc.psum_tensor(f"s5_pg{i}", [128, 512], F32)) for i in range(2)]
            pu = [es.enter_context(nc.psum_tensor(f"s5_pu{i}", [128, 512], F32)) for i in range(2)]
            p = Plan()
            THB = [TA, TB]
            LBF = [LB, LF]
            for hc in range(16):
                for tbg in range(2):
                    p.add("sync", lambda e, hc=hc, tbg=tbg: e.dma_start(
                        out=qxT[:, hc * OS + tbg * 1024: hc * OS + (tbg + 1) * 1024],
                        in_=qx_st[tbg * 1024:(tbg + 1) * 1024, hc * 128:(hc + 1) * 128],
                        transpose=True),
                        waits=(entry() if (hc == 0 and tbg == 0) else []),
                        inc=LA, inck=16)
            t_qxT = LA.n
            ev_h = {}
            st_h = {}
            pe_blk = {}
            ev_blk = {}
            t_wl = {}
            unit = 0

            def emit_w1_batch(ib):
                wb = ib % 2
                ww = ([(sPE, pe_blk[ib - 2]), (sDVE, ev_blk[ib - 2])]
                      if ib >= 2 else [])
                for hc in range(16):
                    p.add("sync", lambda e, ib=ib, hc=hc, wb=wb: e.dma_start(
                        out=w1g[wb][:, hc * 512:(hc + 1) * 512],
                        in_=w1q_st[ib * 512:(ib + 1) * 512, hc * 128:(hc + 1) * 128],
                        transpose=True), waits=(ww if hc == 0 else []), inc=LBF[wb], inck=16)
                    p.add("sync", lambda e, ib=ib, hc=hc, wb=wb: e.dma_start(
                        out=w1u[wb][:, hc * 512:(hc + 1) * 512],
                        in_=w1q_st[DFF + ib * 512: DFF + (ib + 1) * 512,
                                   hc * 128:(hc + 1) * 128],
                        transpose=True), inc=LBF[wb], inck=16)
                p.add("sync", lambda e, ib=ib, wb=wb: e.dma_start(
                    out=sb1g[wb][:], in_=bass.AP(s1d_t, ib * 512, [[0, 128], [1, 512]])),
                    inc=LBF[wb], inck=16)
                t_wl[ib] = p.add("sync", lambda e, ib=ib, wb=wb: e.dma_start(
                    out=sb1u[wb][:], in_=bass.AP(s1d_t, DFF + ib * 512, [[0, 128], [1, 512]])),
                    inc=LBF[wb], inck=16)

            emit_w1_batch(0)
            for ib in range(NIB):
                if ib + 1 < NIB:
                    emit_w1_batch(ib + 1)
                wb = ib % 2
                for tb in range(ONT):
                    gi = unit % 2
                    mwg = [(LBF[wb], t_wl[ib]), (LA, t_qxT)]
                    if unit >= 2:
                        mwg.append((sDVE, ev_h[unit - 2][0]))
                    for hc in range(16):
                        t_mg = p.add("tensor", lambda e, gi=gi, hc=hc, tb=tb, wb=wb: e.matmul(
                            pg[gi][:], qxT[:, hc * OS + tb * 128: hc * OS + (tb + 1) * 128],
                            w1g[wb][:, hc * 512:(hc + 1) * 512],
                            start=(hc == 0), stop=(hc == 15)),
                            waits=(mwg if hc == 0 else []), inc=(sPE if hc == 15 else None))
                    mwu = []
                    if unit >= 2:
                        mwu.append((sDVE, ev_h[unit - 2][1]))
                    for hc in range(16):
                        t_mu = p.add("tensor", lambda e, gi=gi, hc=hc, tb=tb, wb=wb: e.matmul(
                            pu[gi][:], qxT[:, hc * OS + tb * 128: hc * OS + (tb + 1) * 128],
                            w1u[wb][:, hc * 512:(hc + 1) * 512],
                            start=(hc == 0), stop=(hc == 15)),
                            waits=(mwu if hc == 0 else []), inc=(sPE if hc == 15 else None))
                    t_gd = p.add("vector", lambda e, gi=gi, tb=tb, wb=wb: e.scalar_tensor_tensor(
                        out=gd[gi][:], in0=pg[gi][:], scalar=sx_all[:, tb:tb + 1],
                        in1=sb1g[wb][:], op0=ALU.mult, op1=ALU.mult),
                        waits=[(sPE, t_mg)] + ([(sACT, ev_h[unit - 2][2])] if unit >= 2 else []),
                        inc=sDVE)
                    t_sg = p.add("scalar", lambda e, gi=gi: e.activation(
                        out=sgt[gi][:], in_=gd[gi][:], func=AF.Silu),
                        waits=[(sDVE, t_gd)], inc=sACT)
                    t_ud = p.add("vector", lambda e, gi=gi, tb=tb, wb=wb: e.scalar_tensor_tensor(
                        out=ud[gi][:], in0=pu[gi][:], scalar=sx_all[:, tb:tb + 1],
                        in1=sb1u[wb][:], op0=ALU.mult, op1=ALU.mult),
                        waits=[(sPE, t_mu)], inc=sDVE)
                    hw = [(sACT, t_sg), (sDVE, t_ud)]
                    if unit >= 2:
                        hw.append((THB[gi], st_h[unit - 2]))
                    t_h = p.add("vector", lambda e, gi=gi: e.tensor_mul(
                        out=hb[gi][:], in0=sgt[gi][:], in1=ud[gi][:]),
                        waits=hw, inc=sDVE)
                    t_hs = p.add("gpsimd", lambda e, gi=gi, ib=ib, tb=tb: e.dma_start(
                        out=h_st[tb * 128:(tb + 1) * 128, ib * 512:(ib + 1) * 512],
                        in_=hb[gi][:]), waits=[(sDVE, t_h)], inc=THB[gi], inck=16)
                    st_h[unit] = t_hs
                    ev_h[unit] = (t_gd, t_ud, t_sg)
                    unit += 1
                pe_blk[ib] = sPE.n
                ev_blk[ib] = ev_h[unit - 1][1]
            p.emit(nc)

        with contextlib.ExitStack() as es:
            wof = es.enter_context(nc.sbuf_tensor("s4_wof", [128, D], F32))
            wob = es.enter_context(nc.sbuf_tensor("s4_wob", [128, 16 * D], BF16))
            bob = es.enter_context(nc.sbuf_tensor("s4_bob", [128, D], F32))
            otc = [es.enter_context(nc.sbuf_tensor(f"s4_otc{i}", [128, 16 * 128], BF16)) for i in range(2)]
            af = [es.enter_context(nc.sbuf_tensor(f"s4_af{i}", [128, 512], F32)) for i in range(4)]
            ps4 = [es.enter_context(nc.psum_tensor(f"s4_ps{i}", [128, 512], F32)) for i in range(4)]
            BCW = 1408
            hrow = [es.enter_context(nc.sbuf_tensor(f"s5b_h{i}", [128, DFF], BF16)) for i in range(2)]
            qnh = [es.enter_context(nc.sbuf_tensor(f"s5b_qn{i}", [128, DFF], BF16)) for i in range(2)]
            yqh4 = [es.enter_context(nc.sbuf_tensor(f"s5b_yq{i}", [128, BCW], F32)) for i in range(4)]
            sgh4 = [es.enter_context(nc.sbuf_tensor(f"s5b_sg{i}", [128, BCW], BF16)) for i in range(4)]
            tih2 = [es.enter_context(nc.sbuf_tensor(f"s5b_ti{i}", [128, BCW], I32)) for i in range(2)]
            amh = es.enter_context(nc.sbuf_tensor("s5b_am", [128, 2], F32))
            rqh = es.enter_context(nc.sbuf_tensor("s5b_rq", [128, 2], F32))
            p = Plan()
            LOT = [LC, LD]
            TAF = [TA, TB, TC, TD]
            LHR = [LB, LF]
            TNH = [TE, TF]
            p.add("sync", lambda e: e.dma_start(
                out=bob[:], in_=bass.AP(bo_t, 0, [[0, 128], [1, D]])),
                waits=entry(), inc=LE, inck=16)
            t_wc = 0
            for hc in range(16):
                t_wl = p.add("sync", lambda e, hc=hc: e.dma_start(
                    out=wof[:], in_=wo_in[hc * 128:(hc + 1) * 128, :]),
                    waits=([(sDVE, t_wc)] if hc > 0 else []), inc=LE, inck=16)
                t_wc = p.add("vector", lambda e, hc=hc: e.tensor_copy(
                    out=wob[:, hc * D:(hc + 1) * D], in_=wof[:]),
                    waits=[(LE, t_wl)], inc=sDVE)
            otT_r = otT.rearrange("(hc p) q -> p hc q", p=128)
            ev4 = {}
            st4 = {}
            pe_tok4 = 0
            unit = 0
            bq_ylast = {}
            bq_am = {}
            bq_stt = {}
            bq_neg = {}
            bq_store = {}
            for tt in range(ONT):
                bi = tt % 2
                t_l = p.add("sync", lambda e, tt=tt, bi=bi: e.dma_start(
                    out=otc[bi][:], in_=otT_r[:, :, tt * 128:(tt + 1) * 128]),
                    waits=([(sPE, pe_tok4)] if tt >= 2 else []), inc=LOT[bi], inck=16)
                for db in range(4):
                    bank = unit % 4
                    mw = [(LOT[bi], t_l), (sDVE, t_wc)]
                    if unit >= 4:
                        mw.append((sDVE, ev4[unit - 4]))
                    for hc in range(16):
                        t_mm = p.add("tensor", lambda e, bank=bank, bi=bi, hc=hc, db=db: e.matmul(
                            ps4[bank][:], otc[bi][:, hc * 128:(hc + 1) * 128],
                            wob[:, hc * D + db * 512: hc * D + (db + 1) * 512],
                            start=(hc == 0), stop=(hc == 15)),
                            waits=(mw if hc == 0 else []), inc=(sPE if hc == 15 else None))
                    pe_tok4 = t_mm
                    ew = [(sPE, t_mm)]
                    if unit >= 4:
                        ew.append((TAF[bank], st4[unit - 4]))
                    t_ev = p.add("vector", lambda e, bank=bank, db=db: e.tensor_tensor(
                        out=af[bank][:], in0=ps4[bank][:],
                        in1=bob[:, db * 512:(db + 1) * 512], op=ALU.add),
                        waits=ew, inc=sDVE)
                    ev4[unit] = t_ev
                    t_st = p.add("gpsimd", lambda e, bank=bank, tt=tt, db=db: e.dma_start(
                        out=abuf[tt * 128:(tt + 1) * 128, db * 512:(db + 1) * 512],
                        in_=af[bank][:]), waits=[(sDVE, t_ev)], inc=TAF[bank], inck=16)
                    st4[unit] = t_st
                    unit += 1
                hlw = []
                if tt >= 2:
                    hlw = [(sACT, bq_ylast[tt - 2]), (sDVE, bq_am[tt - 2])]
                t_hl = p.add("sync", lambda e, tt=tt, bi=bi: e.dma_start(
                    out=hrow[bi][:], in_=h_st[tt * 128:(tt + 1) * 128, :]),
                    waits=hlw, inc=LHR[bi], inck=16)
                t_am = p.add("vector", lambda e, bi=bi, tt=tt: e.reduce_max(
                    out=amh[:, bi:bi + 1], in_=hrow[bi][:], axis=AX.X,
                    apply_absolute_value=True), waits=[(LHR[bi], t_hl)], inc=sDVE)
                bq_am[tt] = t_am
                t_s = p.add("vector", lambda e, tt=tt, bi=bi: e.tensor_scalar(
                    out=sh_all[:, tt:tt + 1], in0=amh[:, bi:bi + 1], scalar1=INV127,
                    scalar2=EPS_Q, op0=ALU.mult, op1=ALU.max),
                    waits=[(sDVE, t_am)], inc=sDVE)
                t_r = p.add("vector", lambda e, tt=tt, bi=bi: e.reciprocal(
                    out=rqh[:, bi:bi + 1], in_=sh_all[:, tt:tt + 1]),
                    waits=[(sDVE, t_s)], inc=sDVE)
                t_neg = 0
                for c in range(4):
                    sl = slice(c * BCW, (c + 1) * BCW)
                    sgw = [(LHR[bi], t_hl)]
                    if tt >= 1:
                        sgw.append((sDVE, bq_stt[(tt - 1, c)]))
                    t_sg = p.add("scalar", lambda e, bi=bi, sl=sl, c=c: e.activation(
                        out=sgh4[c][:], in_=hrow[bi][:, sl], func=AF.Sign),
                        waits=sgw, inc=sACT)
                    t_y = p.add("scalar", lambda e, bi=bi, sl=sl, c=c: e.activation(
                        out=yqh4[c][:], in_=hrow[bi][:, sl], func=AF.Identity,
                        scale=rqh[:, bi:bi + 1], bias=be_ap),
                        waits=([(sDVE, t_r)] if c == 0 else []), inc=sACT)
                    bq_ylast[tt] = t_y
                    iw = [(sACT, t_y)]
                    if (tt, c - 2) in bq_neg or (tt - 1, c + 2) in bq_neg:
                        prev = bq_neg.get((tt, c - 2), bq_neg.get((tt - 1, c + 2)))
                        iw.append((sACT, prev))
                    t_i = p.add("vector", lambda e, c=c: e.scalar_tensor_tensor(
                        out=tih2[c % 2][:], in0=sgh4[c][:], scalar=al_ap,
                        in1=yqh4[c][:], op0=ALU.mult, op1=ALU.add),
                        waits=iw, inc=sDVE)
                    bq_stt[(tt, c)] = t_i
                    nw = [(sDVE, t_i)]
                    if c == 0 and tt >= 2:
                        nw.append((TNH[bi], bq_store[tt - 2]))
                    t_neg = p.add("scalar", lambda e, bi=bi, sl=sl, c=c: e.activation(
                        out=qnh[bi][:, sl], in_=tih2[c % 2][:],
                        func=AF.Identity, scale=-1.0),
                        waits=nw, inc=sACT)
                    bq_neg[(tt, c)] = t_neg
                bq_store[tt] = p.add("gpsimd", lambda e, tt=tt, bi=bi: e.dma_start(
                    out=qh_st[tt * 128:(tt + 1) * 128, :], in_=qnh[bi][:]),
                    waits=[(sACT, t_neg)], inc=TNH[bi], inck=16)
            p.emit(nc)

        with contextlib.ExitStack() as es:
            qhT = es.enter_context(nc.sbuf_tensor("s5c_qhT", [128, NIC * 1024], BF16, is_transpose_dest=True))
            w2s = [es.enter_context(nc.sbuf_tensor(f"s5c_w2s{i}", [128, NIC * 512], BF16, is_transpose_dest=True)) for i in range(2)]
            sb2 = [es.enter_context(nc.sbuf_tensor(f"s5c_sb2{i}", [128, 512], F32)) for i in range(2)]
            fd = [es.enter_context(nc.sbuf_tensor(f"s5c_fd{i}", [128, 512], F32)) for i in range(2)]
            xa = [es.enter_context(nc.sbuf_tensor(f"s5c_xa{i}", [128, 512], F32)) for i in range(2)]
            ab = [es.enter_context(nc.sbuf_tensor(f"s5c_ab{i}", [128, 512], F32)) for i in range(2)]
            t1b = [es.enter_context(nc.sbuf_tensor(f"s5c_t1{i}", [128, 512], F32)) for i in range(2)]
            ob5 = [es.enter_context(nc.sbuf_tensor(f"s5c_ob{i}", [128, 512], F32)) for i in range(2)]
            pd5 = [es.enter_context(nc.psum_tensor(f"s5c_pd{i}", [128, 512], F32)) for i in range(2)]
            p = Plan()
            LW2 = [LF, LB]
            LXA = [LC, LD]
            TOU = [TE, TF]
            n_out_units = 0
            pe_sb_end = 0
            pe_db = [0, 0]
            db_dve = [0, 0]
            ev_d = {}
            ld_x = {}
            out_st = {}
            for tbg in range(2):
                t0 = tbg * 1024
                qw5 = [(sPE, pe_sb_end)] if tbg > 0 else entry()
                for ic in range(NIC):
                    p.add("sync", lambda e, t0=t0, ic=ic: e.dma_start(
                        out=qhT[:, ic * 1024:(ic + 1) * 1024],
                        in_=qh_st[t0:t0 + 1024, ic * 128:(ic + 1) * 128], transpose=True),
                        waits=(qw5 if ic == 0 else []), inc=LE, inck=16)
                t_qhT = LE.n
                for db in range(4):
                    bi = db % 2
                    ww = [(sPE, pe_db[bi]), (sDVE, db_dve[bi])] if (tbg * 4 + db) >= 2 else []
                    for ic in range(NIC):
                        p.add("sync", lambda e, db=db, ic=ic, bi=bi: e.dma_start(
                            out=w2s[bi][:, ic * 512:(ic + 1) * 512],
                            in_=w2q_st[db * 512:(db + 1) * 512, ic * 128:(ic + 1) * 128],
                            transpose=True), waits=(ww if ic == 0 else []), inc=LW2[bi], inck=16)
                    t_w2 = p.add("sync", lambda e, db=db, bi=bi: e.dma_start(
                        out=sb2[bi][:], in_=bass.AP(s2d_t, db * 512, [[0, 128], [1, 512]])),
                        inc=LW2[bi], inck=16)
                    for tt in range(8):
                        di = n_out_units % 2
                        tok_t = tbg * 8 + tt
                        mw = [(LW2[bi], t_w2), (LE, t_qhT)]
                        if n_out_units >= 2:
                            mw.append((sDVE, ev_d[n_out_units - 2]))
                        for ic in range(NIC):
                            t_md = p.add("tensor", lambda e, di=di, bi=bi, ic=ic, tt=tt: e.matmul(
                                pd5[di][:], qhT[:, ic * 1024 + tt * 128: ic * 1024 + (tt + 1) * 128],
                                w2s[bi][:, ic * 512:(ic + 1) * 512],
                                start=(ic == 0), stop=(ic == NIC - 1)),
                                waits=(mw if ic == 0 else []), inc=(sPE if ic == NIC - 1 else None))
                        pe_db[bi] = t_md
                        pe_sb_end = t_md
                        lw5 = [(sDVE, ld_x[n_out_units - 2])] if n_out_units >= 2 else []
                        t_lx = p.add("sync", lambda e, tok_t=tok_t, db=db, di=di: e.dma_start(
                            out=xa[di][:], in_=x_in[tok_t * 128:(tok_t + 1) * 128,
                                                    db * 512:(db + 1) * 512]),
                            waits=lw5, inc=LXA[di], inck=16)
                        t_la = p.add("sync", lambda e, tok_t=tok_t, db=db, di=di: e.dma_start(
                            out=ab[di][:], in_=abuf[tok_t * 128:(tok_t + 1) * 128,
                                                    db * 512:(db + 1) * 512]),
                            inc=LXA[di], inck=16)
                        t_fd = p.add("vector", lambda e, di=di, bi=bi, tok_t=tok_t: e.scalar_tensor_tensor(
                            out=fd[di][:], in0=pd5[di][:], scalar=sh_all[:, tok_t:tok_t + 1],
                            in1=sb2[bi][:], op0=ALU.mult, op1=ALU.mult),
                            waits=[(sPE, t_md)], inc=sDVE)
                        ev_d[n_out_units] = t_fd
                        db_dve[bi] = t_fd
                        t_t1 = p.add("vector", lambda e, di=di: e.tensor_add(
                            out=t1b[di][:], in0=xa[di][:], in1=ab[di][:]),
                            waits=[(LXA[di], t_la)], inc=sDVE)
                        ld_x[n_out_units] = t_t1
                        ow5 = [(sDVE, t_t1), (sDVE, t_fd)]
                        if n_out_units >= 2:
                            ow5.append((TOU[di], out_st[n_out_units - 2]))
                        t_o = p.add("vector", lambda e, di=di: e.tensor_add(
                            out=ob5[di][:], in0=t1b[di][:], in1=fd[di][:]),
                            waits=ow5, inc=sDVE)
                        t_os = p.add("gpsimd", lambda e, di=di, tok_t=tok_t, db=db: e.dma_start(
                            out=out[tok_t * 128:(tok_t + 1) * 128, db * 512:(db + 1) * 512],
                            in_=ob5[di][:]), waits=[(sDVE, t_o)], inc=TOU[di], inck=16)
                        out_st[n_out_units] = t_os
                        n_out_units += 1
            tot_e = TE.n
            tot_f = TF.n
            p.add("gpsimd", lambda e, tot_e=tot_e: e.wait_ge(TE.h, tot_e))
            p.add("gpsimd", lambda e, tot_f=tot_f: e.wait_ge(TF.h, tot_f))
            p.emit(nc)

        if DEBUG:
            p = Plan()
            for nm, (dst, src) in dbg.items():
                p.add("gpsimd", lambda e, dst=dst, src=src: e.dma_start(out=dst, in_=src),
                      inc=TE, inck=16)
            tot = TE.n
            p.add("gpsimd", lambda e, tot=tot: e.wait_ge(TE.h, tot))
            p.emit(nc)

    return nc


def _masks_np(half):
    m = np.zeros((6, 128, 512), dtype=np.float32)
    i = np.arange(128)[:, None]
    j = np.arange(512)[None, :]
    for d in range(4):
        m[d] = (i <= j - d * 128).astype(np.float32)
    m[4] = float(half)
    m[5] = float(half)
    return m.astype(ml_dtypes.bfloat16)


def _own_rows(half):
    return np.concatenate([np.arange(512 * (2 * k + half), 512 * (2 * k + half) + 512)
                           for k in range(4)])


def kernel(L0_attn_x, L0_attn_Wqkv, L0_attn_bqkv, L0_attn_Wo, L0_attn_bo,
           L0_ffn_w_gate_up, L0_ffn_w_down):
    x = np.ascontiguousarray(np.asarray(L0_attn_x, dtype=np.float32))
    wqkv = np.ascontiguousarray(np.asarray(L0_attn_Wqkv, dtype=np.float32))
    bqkv = np.ascontiguousarray(np.asarray(L0_attn_bqkv, dtype=np.float32))
    wo = np.ascontiguousarray(np.asarray(L0_attn_Wo, dtype=np.float32))
    bo = np.ascontiguousarray(np.asarray(L0_attn_bo, dtype=np.float32))
    w1 = np.ascontiguousarray(np.asarray(L0_ffn_w_gate_up, dtype=np.float32))
    w2 = np.ascontiguousarray(np.asarray(L0_ffn_w_down, dtype=np.float32))

    bqkv2 = np.ascontiguousarray(bqkv.reshape(48, 128).T)
    bo1 = np.ascontiguousarray(bo.reshape(1, D))
    masks = [_masks_np(h) for h in range(2)]
    rows = [_own_rows(h) for h in range(2)]

    nc = build_nc()
    in_maps = []
    for c in range(NCORE):
        b, h = c // 2, c % 2
        xp = np.ascontiguousarray(
            np.concatenate([x[b][rows[h]], x[b][rows[1 - h]]], axis=0))
        in_maps.append(dict(
            x=xp, wqkv=wqkv, bqkv2=bqkv2, wo=wo,
            bo=bo1, w1=w1, w2=w2, masks=masks[h]))
    res = run_bass_kernel_spmd(nc, in_maps, core_ids=list(range(NCORE)))
    out_full = np.empty((B, S, D), dtype=np.float32)
    for c in range(NCORE):
        b, h = c // 2, c % 2
        out_full[b][rows[h]] = np.asarray(res.results[c]["out"])
    return out_full
